# version 32
# baseline (speedup 1.0000x reference)
import numpy as np
import ml_dtypes

import concourse.bass as bass
import concourse.mybir as mybir
from concourse.bass_utils import run_bass_kernel_spmd

F32 = mybir.dt.float32
BF16 = mybir.dt.bfloat16
AF = mybir.ActivationFunctionType
ALU = mybir.AluOpType

D = 1024
S = 4096
NT = 16
NT2 = 8
TR_LIMIT = 10**9
TR_NOEVAC = False
TR_ALTSRC = False
ATT_HEADS = 16
NO_NORM = False
RESID_MODE = 4
XRES_SIMPLE = False
TR_FAKESRC = False
EPS = 1e-5

ENGINES = ("pe", "act", "dve", "pool", "sp")
SEMS = ("pe", "act", "dve", "poolc", "ldx", "ldw", "cld", "spl", "ws", "w12",
        "ld2", "ld2g", "xres", "outs", "trd", "p1")
DMA_SEMS = {"ldx", "ldw", "cld", "spl", "ws", "w12", "ld2", "ld2g", "xres", "outs", "trd"}


class Planner:
    def __init__(self):
        self.entries = {e: [] for e in ENGINES}
        self.cnt = {s: 0 for s in SEMS}

    def op(self, eng, fn, waits=(), inc=None, tag=""):
        waits = [(s, c) for (s, c) in waits if c > 0]
        incs = []
        if inc is not None:
            d = 16 if inc in DMA_SEMS else 1
            self.cnt[inc] += d
            incs.append((inc, d))
        self.entries[eng].append([fn, waits, incs, tag])

    def dma(self, eng, sem, fn, waits=(), tag=""):
        self.op(eng, fn, waits, inc=sem, tag=tag)

    def evt(self, sem):
        return (sem, self.cnt[sem])

    def extra_inc(self, eng, sem):
        d = 16 if sem in DMA_SEMS else 1
        self.cnt[sem] += d
        self.entries[eng][-1][2].append((sem, d))


def build_program(debug=False, p1_tiles=NT, do_pass2=True, sections=('proj','attn','wo')):
    nc = bass.Bass("TRN2", target_bir_lowering=False, detect_race_conditions=False)

    dr = {}
    def din(name, shape, dt=F32):
        dr[name] = nc.dram_tensor(name, list(shape), dt, kind="ExternalInput").ap()
        return dr[name]

    xt = din("xt", (D, S))
    din("wq", (D, D)); din("wk", (D, D)); din("wv", (D, D)); din("wo", (D, D))
    din("fwt", (D, D))
    w1r = din("w1r", (32, 8, 128, 128))
    w2r = din("w2r", (32, 8, 128, 128))
    din("dp", (128, D))
    din("pp", (128, 128), BF16)
    din("idn", (128, 128), BF16)
    din("masks", (128, 4 * 256), BF16)
    din("bvb_in", (128, D))
    for nm in ("bq2", "bk2", "bo2", "g12", "b12", "g22", "b22", "c22"):
        din(nm, (128, 8))
    din("c12", (128, 32))
    din("ones128", (128, 1))
    din("ones1x128", (1, 128))

    outT = nc.dram_tensor("outT", [D, S], F32, kind="ExternalOutput").ap()
    sk = "ExternalOutput" if debug else "Internal"
    hts = nc.dram_tensor("hts", [NT, 128, 2048], F32, kind=sk).ap()
    yts = nc.dram_tensor("yts", [NT, 128, 2048], F32, kind=sk).ap()
    tdbg = nc.dram_tensor("tdbg", [NT2, 128, 4096], F32, kind=sk).ap() if debug else None
    udbg = nc.dram_tensor("udbg", [16, 128, 8192], BF16, kind=sk).ap() if debug else None
    hdbg = nc.dram_tensor("hdbg", [NT2, 128, 4096], F32, kind=sk).ap() if debug else None
    mdbg = nc.dram_tensor("mdbg", [NT2, 128, 1024], F32, kind=sk).ap() if debug else None
    w1s = nc.dram_tensor("w1s", [32, 128, 1024], BF16, kind="Internal").ap()
    w2s = nc.dram_tensor("w2s", [32, 128, 1024], BF16, kind="Internal").ap()

    P = Planner()
    sems = {s: nc.alloc_semaphore(name="s_" + s) for s in SEMS}

    SB_LIMIT = 229370
    cursor = [18432]
    def salloc(name, shape, dt, region_start=None):
        if region_start is not None:
            cursor[0] = region_start
        nbytes = shape[1] * mybir.dt.size(dt)
        off = (cursor[0] + 31) // 32 * 32
        cursor[0] = off + nbytes
        assert cursor[0] <= SB_LIMIT, f"SBUF overflow at {name}: {cursor[0]}"
        return nc.alloc_sbuf_tensor_at(name, list(shape), dt, offset=off)

    bq_s = salloc("bq_s", [128, 8], F32)
    bk_s = salloc("bk_s", [128, 8], F32)
    bo_s = salloc("bo_s", [128, 8], F32)
    g1_s = salloc("g1_s", [128, 8], F32)
    b1_s = salloc("b1_s", [128, 8], F32)
    g2_s = salloc("g2_s", [128, 8], F32)
    b2_s = salloc("b2_s", [128, 8], F32)
    c2_s = salloc("c2_s", [128, 8], F32)
    c1_s = salloc("c1_s", [128, 32], F32)
    on128 = salloc("on128", [128, 1], F32)
    on1x = salloc("on1x", [1, 128], F32)
    cst = salloc("cst", [128, 4], F32)
    idn_b = salloc("idn_b", [128, 128], BF16)
    pp_b = salloc("pp_b", [128, 128], BF16)
    mask_b = salloc("mask_b", [128, 1024], BF16)
    SHARED_END = cursor[0]

    wq_b = salloc("wq_b", [128, 8192], BF16)
    wk_b = salloc("wk_b", [128, 8192], BF16)
    wv_b = salloc("wv_b", [128, 8192], BF16)
    wo_b = salloc("wo_b", [128, 8192], BF16)
    fw_b = salloc("fw_b", [128, 8192], BF16)
    dp_b = salloc("dp_b", [128, 1024], BF16)
    bvb = salloc("bvb", [128, 1024], BF16)
    wstg = [salloc("wstg0", [128, 1024], BF16)] * 2
    xT = [salloc(f"xT{i}", [128, 2048], F32) for i in range(2)]
    xres = [salloc(f"xres{i}", [128, 512], F32) for i in range(2)]
    sq_glob = salloc("sq", [128, 2048], F32)
    rows = salloc("rows", [1, 1280], F32)
    mub = [salloc(f"mub{i}", [128, 256], BF16) for i in range(2)]
    rbf = [salloc(f"rbf{i}", [128, 256], F32) for i in range(2)]
    mu2b = salloc("mu2b", [128, 256], BF16)
    r2b = salloc("r2b", [128, 256], F32)
    xnTb = salloc("xnTb", [128, 2048], BF16)
    qTb = [salloc(f"qTb{i}", [128, 2048], BF16) for i in range(2)]
    kTb = [salloc(f"kTb{i}", [128, 2048], BF16) for i in range(3)]
    vaug = [salloc(f"vaug{i}", [128, 2080], BF16) for i in range(3)]
    expSb = [salloc(f"expSb{i}", [128, 1536], BF16) for i in range(3)]
    mexp = [salloc(f"mexp{i}", [128, 1024], BF16) for i in range(3)]
    rcpt = salloc("rcpt", [128, 32], F32)
    attn_out = salloc("attn_out", [128, 2048], BF16)
    aoTb = salloc("aoTb", [128, 2048], BF16)
    tTb = salloc("tTb", [128, 2048], F32)
    rs1 = salloc("rs1", [128, 512], F32)
    hbuf = [salloc(f"hbuf{i}", [128, 512], F32) for i in range(2)]
    ybuf = [salloc(f"ybuf{i}", [128, 512], F32) for i in range(2)]
    pooledTb = salloc("pooledTb", [128, 256], BF16)
    av_sb = [salloc(f"av_sb{i}", [128, 130], F32) for i in range(2)]

    w1res = salloc("w1res", [128, 32768], BF16, region_start=SHARED_END)
    w2res = salloc("w2res", [128, 32768], BF16)
    uTb = salloc("uTb", [128, 8192], BF16)
    hTb2 = [salloc(f"hTb2_{i}", [128, 2048], BF16) for i in range(2)]
    hq = [salloc(f"hq{i}", [128, 256], F32) for i in range(4)]
    tT2b = [salloc(f"tT2_{i}", [128, 2048], F32) for i in range(2)]
    sqp2 = salloc("sqp2", [128, 2048], F32)
    rows3 = salloc("rows3", [1, 1280], F32)
    mu3b2 = [salloc(f"mu3b{i}", [128, 256], BF16) for i in range(2)]
    r3b2 = [salloc(f"r3b{i}", [128, 256], F32) for i in range(2)]
    yqf = salloc("yqf", [128, 2048], F32)
    xq = [salloc(f"xq{i}", [128, 256], F32) for i in range(2)]
    thb8 = sqp2
    rs3 = salloc("rs3", [128, 512], F32)
    outb = [salloc(f"outb{i}", [128, 256], F32) for i in range(2)]

    PS5 = nc.alloc_psum_tensor("PS5", [128, 512], F32)
    PS0 = nc.alloc_psum_tensor("PS0", [128, 1024], F32)
    PS1 = nc.alloc_psum_tensor("PS1", [128, 1024], F32)
    PS2 = nc.alloc_psum_tensor("PS2", [128, 512], F32)
    PS3 = nc.alloc_psum_tensor("PS3", [128, 512], F32)
    PS4 = nc.alloc_psum_tensor("PS4", [128, 512], F32)
    PA, PB, P4, P5, P6, P7t = PS0, PS1, PS2, PS3, PS4, PS5

    for nm, tile in (("bq2", bq_s), ("bk2", bk_s), ("bo2", bo_s), ("g12", g1_s),
                     ("b12", b1_s), ("g22", g2_s), ("b22", b2_s), ("c22", c2_s),
                     ("c12", c1_s), ("ones128", on128), ("ones1x128", on1x),
                     ("idn", idn_b), ("pp", pp_b), ("masks", mask_b)):
        P.dma("sp", "ldw", lambda sp, a=dr[nm], t=tile: sp.dma_start(out=t[:, :], in_=a),
              tag="ldc_" + nm)
    consts_evt = P.evt("ldw")

    wevt = {}
    for name, dst in (("wq", wq_b), ("wk", wk_b), ("wv", wv_b), ("fw", fw_b), ("wo", wo_b)):
        src = dr[{"fw": "fwt"}.get(name, name)]
        for d in range(8):
            P.dma("pool", "cld", lambda gp, s=src, t=dst, d=d: gp.dma_start(
                out=t[:, d * 1024:(d + 1) * 1024], in_=s[d * 128:(d + 1) * 128, :]),
                tag=f"cast_{name}{d}")
        wevt[name] = P.evt("cld")
    P.dma("pool", "cld", lambda gp: gp.dma_start(out=dp_b[:, :], in_=dr["dp"]), tag="cast_dp")
    wevt["dp"] = P.evt("cld")
    P.dma("pool", "cld", lambda gp: gp.dma_start(out=bvb[:, :], in_=dr["bvb_in"]), tag="cast_bv")
    wevt["bv"] = P.evt("cld")

    for b in range(3):
        for i in range(2):
            for h in range(16):
                col = i * 1040 + h * 65 + 64
                P.op("dve", lambda v, b=b, c=col: v.memset(vaug[b][:, c:c + 1], 1.0),
                     tag="ones_v")
    P.op("dve", lambda v: v.memset(cst[:, 0:1], EPS), inc="dve", tag="eps")
    eps_evt = P.evt("dve")

    def load_xT(t, clo=0, chi=8):
        def fn(sp, t=t, clo=clo, chi=chi):
            src = xt[clo * 128:chi * 128,
                     t * 256:(t + 1) * 256].rearrange("(c p) n -> p c n", p=128)
            dst = xT[t % 2][:, clo * 256:chi * 256].rearrange(
                "p (c n) -> p c n", n=256)
            return sp.dma_start(out=dst, in_=src)
        return fn
    ldx_evt = {}
    P.dma("sp", "ldx", load_xT(0, 0, 4), tag="ldx0a")
    P.dma("act", "ldx", load_xT(0, 4, 8), tag="ldx0b")
    ldx_evt[0] = P.evt("ldx")

    precast_jobs = [("w1", c) for c in range(32)] + [("w2", c) for c in range(32)]
    ws_evt = {}
    stg_free = [("ws", 0), ("ws", 0)]

    def plan_precast(job_idx):
        which, c = precast_jobs[job_idx]
        buf = job_idx % 2
        src = w1r if which == "w1" else w2r
        dst = w1s if which == "w1" else w2s
        def cast_fn(gp, src=src, c=c, buf=buf):
            return gp.dma_start(out=wstg[buf][:, :].rearrange("p (a f) -> p a f", f=128),
                                in_=src[c].rearrange("a p f -> p a f"))
        P.dma("pool", "cld", cast_fn, waits=[stg_free[buf]], tag=f"pcc_{which}{c}")
        cast_evt = P.evt("cld")
        def store_fn(sp, dst=dst, c=c, buf=buf):
            return sp.dma_start(out=dst[c], in_=wstg[buf][:, :])
        P.dma("sp", "ws", store_fn, waits=[cast_evt], tag=f"pcs_{which}{c}")
        ws_evt[(which, c)] = P.evt("ws")
        stg_free[0] = ws_evt[(which, c)]
        stg_free[1] = ws_evt[(which, c)]

    xn_evt, mu_evt, qT_evt, kT_evt, v_evt = {}, {}, {}, {}, {}
    p6_free = ("dve", 0)
    rows_free = ("pe", 0)
    hb_free = [("spl", 0), ("spl", 0)]
    yb_free = [("spl", 0), ("spl", 0)]
    xres_free = [("dve", 0), ("dve", 0)]
    slots = [[P4, 0, ("act", 0)], [P5, 0, ("act", 0)]]
    slot_rr = [0]
    def acquire_slot():
        sl = slots[slot_rr[0] % 2]
        slot_rr[0] += 1
        return sl
    expA_evt = [("act", 0), ("act", 0), ("act", 0)]
    expB_evt = [("act", 0), ("act", 0), ("act", 0)]
    av_evt = [("pe", 0), ("pe", 0), ("pe", 0)]
    mexp_free = [("pe", 0), ("pe", 0), ("pe", 0)]
    rcp_free = {}
    tr_done_prev = ("pe", 0)
    attn_out_war = ("trd", 0)
    aoTb_war = ("pe", 0)
    scoresA_free = ("act", 0)
    scoresB_free = ("act", 0)
    av_region_free = ("act", 0)
    p7_free = [("dve", 0), ("dve", 0), ("dve", 0)]

    def act_fence(rows_t, tag):
        P.op("act", lambda a, rows_t=rows_t: a.activation(
            out=rows_t[0:1, 1279:1280] if rows_t.shape[1] == 1280 else rows_t[0:1, 2047:2048],
            in_=cst[0:1, 0:1], func=AF.Identity), inc="act", tag=tag)
        return P.evt("act")

    def plan_ln_stats_p1(src_tile, rows_t, rows_waits, sq_evt_out,
                         tagp, extra_stats_waits, sq_buf=None, chunk_evts=None):
        sq = sq_buf if sq_buf is not None else sq_glob
        """LN statistics over 8 chunks of (128,256) from src_tile -> broadcast
        mu into mu_dst (bf16), rstd into r_dst (f32) via P6. Returns evt of the
        mu/r evac (dve). With chunk_evts, the caller has already produced both
        src chunk c and its square in sq chunk c by chunk_evts[c]; the sums
        then pipeline per chunk behind the producer instead of waiting for the
        whole tile."""
        if chunk_evts is None:
            for c in range(8):
                P.op("dve", lambda v, c=c: v.tensor_mul(
                    sq[:, c * 256:(c + 1) * 256], src_tile[:, c * 256:(c + 1) * 256],
                    src_tile[:, c * 256:(c + 1) * 256]),
                    waits=(sq_evt_out if c == 0 else []), tag=f"{tagp}sq{c}")
            P.extra_inc("dve", "dve")
            sqr_evt = P.evt("dve")
        for c in range(8):
            w = (extra_stats_waits if c == 0 else [])
            if chunk_evts is not None:
                w = w + [chunk_evts[c]]
            P.op("pe", lambda pe, c=c: pe.matmul(
                P6[0:1, 0:256], on128[:, 0:1], src_tile[:, c * 256:(c + 1) * 256],
                start=(c == 0), stop=(c == 7)),
                waits=w, tag=f"{tagp}sums{c}")
        P.extra_inc("pe", "pe")
        for c in range(8):
            w = []
            if chunk_evts is None and c == 0:
                w = [sqr_evt]
            P.op("pe", lambda pe, c=c: pe.matmul(
                P6[0:1, 256:512], on128[:, 0:1], sq[:, c * 256:(c + 1) * 256],
                start=(c == 0), stop=(c == 7)),
                waits=w, tag=f"{tagp}sumsq{c}")
        P.extra_inc("pe", "pe")
        stats_evt = P.evt("pe")
        P.op("dve", lambda v: v.tensor_copy(rows_t[0:1, 512:1024], P6[0:1, 0:512]),
             waits=[stats_evt] + rows_waits, tag=f"{tagp}rowcp")
        P.op("dve", lambda v: v.tensor_scalar_mul(rows_t[0:1, 0:256], rows_t[0:1, 512:768], 1.0 / D))
        P.op("dve", lambda v: v.tensor_scalar_mul(rows_t[0:1, 1024:1280], rows_t[0:1, 768:1024], 1.0 / D))
        P.op("dve", lambda v: v.tensor_mul(rows_t[0:1, 768:1024], rows_t[0:1, 0:256], rows_t[0:1, 0:256]))
        P.op("dve", lambda v: v.tensor_sub(rows_t[0:1, 512:768], rows_t[0:1, 1024:1280], rows_t[0:1, 768:1024]))
        P.op("dve", lambda v: v.tensor_scalar_add(rows_t[0:1, 768:1024], rows_t[0:1, 512:768], EPS),
             inc="dve")
        var_evt = P.evt("dve")
        P.op("act", lambda a: a.activation(out=rows_t[0:1, 1024:1280], in_=rows_t[0:1, 768:1024],
                                           func=AF.Ln),
             waits=[var_evt], tag=f"{tagp}ln")
        P.op("act", lambda a: a.activation(out=rows_t[0:1, 256:512], in_=rows_t[0:1, 1024:1280],
                                           func=AF.Exp, scale=-0.5), inc="act")
        rr_evt = act_fence(rows_t, f"{tagp}fence")
        return rr_evt, stats_evt

    def plan_ln_stats(src_tile, rows_t, rows_waits, mu_dst, r_dst, sq_evt_out,
                      tagp, extra_stats_waits, sq_buf=None, chunk_evts=None):
        rr_evt, _ = plan_ln_stats_p1(src_tile, rows_t, rows_waits, sq_evt_out,
                                     tagp, extra_stats_waits, sq_buf, chunk_evts)
        return plan_ln_bcast(rows_t, mu_dst, r_dst, rr_evt, [], tagp)

    def plan_ln_bcast(rows_t, mu_dst, r_dst, rr_evt, bcast_extra, tagp):
        P.op("pe", lambda pe, rows_t=rows_t: pe.matmul(
            P6[:, 0:512], on1x[0:1, :], rows_t[0:1, 0:512], start=True, stop=True),
             waits=[rr_evt] + bcast_extra, inc="pe", tag=f"{tagp}bcast")
        bc_evt = P.evt("pe")
        P.op("dve", lambda v, mu_dst=mu_dst: v.tensor_copy(mu_dst[:, :], P6[:, 0:256]),
             waits=[bc_evt], tag=f"{tagp}mue")
        P.op("dve", lambda v, r_dst=r_dst: v.tensor_copy(r_dst[:, :], P6[:, 256:512]),
             inc="dve")
        return P.evt("dve"), bc_evt

    for s in range(p1_tiles + 1):
        t, u = s, s - 1
        if t >= p1_tiles:
            t = NT

        for j in range(4):
            ji = s * 4 + j
            if ji < len(precast_jobs):
                plan_precast(ji)

        if t < NT:
            if t + 1 < NT:
                P.dma("sp", "ldx", load_xT(t + 1),
                      waits=[xn_evt.get(t - 1, ("dve", 0))], tag=f"ldx{t+1}")
                ldx_evt[t + 1] = P.evt("ldx")

            mu_evt[t], bc1_evt = plan_ln_stats(
                xT[t % 2], rows, [rows_free], mub[t % 2], rbf[t % 2],
                [ldx_evt[t]], f"L1_{t}_",
                [ldx_evt[t], consts_evt, p6_free])
            rows_free = bc1_evt
            p6_free = mu_evt[t]
            for c in range(4):
                P.op("dve", lambda v, t=t, c=c: v.tensor_sub(
                    sq_glob[:, c * 256:(c + 1) * 256], xT[t % 2][:, c * 256:(c + 1) * 256],
                    mub[t % 2][:, :]), tag=f"xns{t}_{c}")
                P.op("dve", lambda v, t=t, c=c: v.tensor_mul(
                    xnTb[:, c * 256:(c + 1) * 256], sq_glob[:, c * 256:(c + 1) * 256],
                    rbf[t % 2][:, :]))
            P.extra_inc("dve", "dve")
            for c in range(4, 8):
                P.op("pool", lambda gp, t=t, c=c: gp.tensor_sub(
                    sq_glob[:, c * 256:(c + 1) * 256], xT[t % 2][:, c * 256:(c + 1) * 256],
                    mub[t % 2][:, :]),
                    waits=([bc1_evt] if c == 4 else []), tag=f"xnp{t}_{c}")
                P.op("pool", lambda gp, t=t, c=c: gp.tensor_mul(
                    xnTb[:, c * 256:(c + 1) * 256], sq_glob[:, c * 256:(c + 1) * 256],
                    rbf[t % 2][:, :]))
            P.extra_inc("pool", "poolc")
            xnp_evt = P.evt("poolc")
            P.op("dve", lambda v: v.engine_nop(), waits=[xnp_evt], inc="dve",
                 tag=f"xnf{t}")
            xn_evt[t] = P.evt("dve")

            for name, wb, bias_s, dst in (("q", wq_b, bq_s, qTb[t % 2]),
                                          ("k", wk_b, bk_s, kTb[t % 3])):
                for g in range(8):
                    sl = acquire_slot()
                    pt, coff, free_evt = sl
                    for d in range(8):
                        P.op("pe", lambda pe, wb=wb, g=g, d=d, pt=pt, coff=coff: pe.matmul(
                            pt[:, coff:coff + 256],
                            wb[:, d * 1024 + g * 128: d * 1024 + (g + 1) * 128],
                            xnTb[:, d * 256:(d + 1) * 256],
                            start=(d == 0), stop=(d == 7)),
                            waits=([xn_evt[t], wevt["wq" if name == "q" else "wk"],
                                    free_evt, tr_done_prev] if d == 0 else []),
                            tag=f"{name}T{t}_{g}_{d}")
                    P.extra_inc("pe", "pe")
                    grp_evt = P.evt("pe")
                    P.op("dve", lambda v, dst=dst, g=g, pt=pt, coff=coff, bs=bias_s: v.tensor_scalar_add(
                        dst[:, g * 256:(g + 1) * 256], pt[:, coff:coff + 256],
                        bs[:, g:g + 1]),
                        waits=[grp_evt, consts_evt], inc="dve", tag=f"{name}Te{t}_{g}")
                    sl[2] = P.evt("dve")
                if name == "q":
                    qT_evt[t] = P.evt("dve")
                else:
                    kT_evt[t] = P.evt("dve")

            for i in range(2):
                for j in range(2):
                    fb = (i * 2 + j) % 2
                    pt = (P4, P5)[fb]
                    for d in range(8):
                        P.op("pe", lambda pe, i=i, j=j, d=d, pt=pt: pe.matmul(
                            pt[:, 0:512],
                            xnTb[:, d * 256 + i * 128: d * 256 + i * 128 + 128],
                            wv_b[:, d * 1024 + j * 512: d * 1024 + (j + 1) * 512],
                            start=(d == 0), stop=(d == 7)),
                            waits=([xn_evt[t], wevt["wv"], slots[fb][2]] if d == 0 else []),
                            tag=f"v{t}_{i}{j}_{d}")
                    P.extra_inc("pe", "pe")
                    grp_evt = P.evt("pe")
                    def vevac(v, t=t, i=i, j=j, pt=pt):
                        outap = vaug[t % 3][:, i * 1040 + j * 520: i * 1040 + (j + 1) * 520] \
                            .rearrange("p (h c) -> p h c", c=65)[:, :, 0:64]
                        in0 = pt[:, 0:512].rearrange("p (h c) -> p h c", c=64)
                        in1 = bvb[:, j * 512:(j + 1) * 512].rearrange("p (h c) -> p h c", c=64)
                        return v.tensor_tensor(outap, in0, in1, ALU.add)
                    P.op("dve", vevac, waits=[grp_evt, wevt["bv"], tr_done_prev],
                         inc="dve", tag=f"vev{t}_{i}{j}")
                    slots[fb][2] = P.evt("dve")
            v_evt[t] = P.evt("dve")

            for d in range(8):
                P.op("pe", lambda pe, d=d: pe.matmul(
                    P6[:, 0:256], pp_b[:, :], xnTb[:, d * 256:(d + 1) * 256],
                    start=(d == 0), stop=(d == 7)),
                    waits=([xn_evt[t], consts_evt, p6_free] if d == 0 else []),
                    tag=f"pool{t}_{d}")
            P.extra_inc("pe", "pe")
            pool_evt = P.evt("pe")
            P.op("dve", lambda v: v.tensor_copy(pooledTb[:, :], P6[:, 0:256]),
                 waits=[pool_evt], inc="dve", tag=f"poole{t}")
            p6_free = P.evt("dve")
            pooled_evt = p6_free

            for g in range(8):
                sl = acquire_slot()
                pt, coff, free_evt = sl
                for d in range(8):
                    P.op("pe", lambda pe, g=g, d=d, pt=pt, coff=coff: pe.matmul(
                        pt[:, coff:coff + 256],
                        fw_b[:, d * 1024 + g * 128: d * 1024 + (g + 1) * 128],
                        xnTb[:, d * 256:(d + 1) * 256],
                        start=(d == 0), stop=False),
                        waits=([xn_evt[t], wevt["fw"], wevt["dp"], pooled_evt, free_evt]
                               if d == 0 else []),
                        tag=f"y{t}_{g}_{d}")
                P.op("pe", lambda pe, g=g, pt=pt, coff=coff: pe.matmul(
                    pt[:, coff:coff + 256], dp_b[:, g * 128:(g + 1) * 128],
                    pooledTb[:, :], start=False, stop=True), inc="pe", tag=f"yd{t}_{g}")
                grp_evt = P.evt("pe")
                yb = (g // 2) % 2
                P.op("dve", lambda v, g=g, pt=pt, coff=coff, yb=yb: v.tensor_copy(
                    ybuf[yb][:, (g % 2) * 256:(g % 2) * 256 + 256], pt[:, coff:coff + 256]),
                    waits=[grp_evt] + ([yb_free[yb]] if g % 2 == 0 else []),
                    inc="dve", tag=f"yev{t}_{g}")
                sl[2] = P.evt("dve")
                if g % 2 == 1:
                    yev = P.evt("dve")
                    P.dma("pool", "spl", lambda gp, t=t, g=g, yb=yb: gp.dma_start(
                        out=yts[t, :, (g - 1) * 256:(g + 1) * 256], in_=ybuf[yb][:, :]),
                        waits=[yev], tag=f"yspill{t}_{g}")
                    yb_free[yb] = P.evt("spl")

        if u >= 0 and 'attn' in sections:
            valid_kc = [kc for kc in range(6) if 0 <= (u - 1 + kc // 2) <= 15]
            masked_kc = [kc for kc in valid_kc if kc in (0, 1, 4, 5)]
            mslot = {0: 0, 1: 1, 4: 2, 5: 3}
            mk_evt = {}

            def plan_av(h):
                nonlocal av_region_free
                ping = h % 3
                for q2 in range(2):
                    n = len(valid_kc)
                    for ii, kc in enumerate(valid_kc):
                        b = (u - 1 + kc // 2) % 3
                        if kc in masked_kc:
                            lhsT = mexp[ping][:, mslot[kc] * 256 + q2 * 128:
                                              mslot[kc] * 256 + q2 * 128 + 128]
                        else:
                            lhsT = expSb[ping][:, kc * 256 + q2 * 128:
                                               kc * 256 + q2 * 128 + 128]
                        rhs = vaug[b][:, (kc % 2) * 1040 + h * 65:(kc % 2) * 1040 + h * 65 + 65]
                        P.op("pe", lambda pe, lhsT=lhsT, rhs=rhs, q2=q2, ii=ii, n=n: pe.matmul(
                            P6[:, 256 + q2 * 65:256 + q2 * 65 + 65], lhsT, rhs,
                            start=(ii == 0), stop=(ii == n - 1)),
                            waits=([mk_evt[h], expB_evt[ping], av_region_free, p6_free,
                                    v_evt[min(u + 1, p1_tiles - 1)]] if (ii == 0 and q2 == 0) else []),
                            tag=f"av{u}_{h}_{q2}_{kc}")
                    P.extra_inc("pe", "pe")
                avh_evt = P.evt("pe")
                av_evt[ping] = avh_evt
                mexp_free[ping] = avh_evt
                sb = av_sb[h % 2]
                P.op("dve", lambda v, sb=sb: v.tensor_copy(sb[:, :], P6[:, 256:386]),
                     waits=[avh_evt], inc="dve", tag=f"avev{u}_{h}")
                evac_evt = P.evt("dve")
                av_region_free = ("dve", evac_evt[1] + 1)
                for q2 in range(2):
                    col = (h % 3) * 2 + q2
                    P.op("dve", lambda v, q2=q2, col=col, sb=sb: v.reciprocal(
                        rcpt[:, col * 4:col * 4 + 1], sb[:, q2 * 65 + 64:q2 * 65 + 65]),
                        waits=[rcp_free.get(col, ("dve", 0))], inc="dve",
                        tag=f"rcp{u}_{h}_{q2}")
                    rc_evt = P.evt("dve")
                    P.op("dve", lambda v, h=h, q2=q2, col=col, sb=sb: v.tensor_scalar(
                        attn_out[:, q2 * 1024 + h * 64: q2 * 1024 + (h + 1) * 64],
                        sb[:, q2 * 65:q2 * 65 + 64],
                        rcpt[:, col * 4:col * 4 + 1], None, ALU.mult),
                        waits=[rc_evt, attn_out_war], inc="dve", tag=f"nrm{u}_{h}_{q2}")
                    norm_evt[(h, q2)] = P.evt("dve")
                    rcp_free[col] = norm_evt[(h, q2)]

            def plan_tr(c):
                for q2 in range(2):
                    if c * 2 + q2 >= TR_LIMIT:
                        continue
                    P.dma("sp", "trd", lambda sp, c=c, q2=q2: sp.dma_start_transpose(
                        aoTb[:, c * 256 + q2 * 128: c * 256 + q2 * 128 + 128],
                        attn_out[:, q2 * 1024 + c * 128: q2 * 1024 + (c + 1) * 128]),
                        waits=[norm_evt[(2 * c + 1, q2)], norm_evt[(2 * c, q2)],
                               aoTb_war],
                        tag=f"tr{u}_{c}_{q2}")

            def plan_tr_flush():
                pass

            do_tr = 'tr' in sections or 'wo' in sections
            norm_evt = {}
            for h in range(ATT_HEADS):
                ping = h % 3
                for part in ("A", "B"):
                    kcs = [kc for kc in valid_kc if (kc < 3) == (part == "A")]
                    pt = PA if part == "A" else PB
                    free_evt = scoresA_free if part == "A" else scoresB_free
                    for kc in kcs:
                        hp = (h % 2) * 64
                        b = (u - 1 + kc // 2) % 3
                        lhsT = kTb[b][hp:hp + 64,
                                      (h // 2) * 256 + (kc % 2) * 128:
                                      (h // 2) * 256 + (kc % 2) * 128 + 128]
                        rhs = qTb[u % 2][hp:hp + 64, (h // 2) * 256:(h // 2) * 256 + 256]
                        P.op("pe", lambda pe, lhsT=lhsT, rhs=rhs, kc=kc, pt=pt: pe.matmul(
                            pt[:, (kc % 3) * 256:(kc % 3) * 256 + 256], lhsT, rhs,
                            start=True, stop=True),
                            waits=([qT_evt[u], kT_evt[min(u + 1, p1_tiles - 1)], free_evt]
                                   if kc == kcs[0] else []),
                            tag=f"sc{u}_{h}{part}_{kc}")
                    P.extra_inc("pe", "pe")
                    sc_evt = P.evt("pe")
                    lo = min(kc % 3 for kc in kcs)
                    hi = max(kc % 3 for kc in kcs) + 1
                    off = 0 if part == "A" else 768
                    P.op("act", lambda a, pt=pt, lo=lo, hi=hi, off=off, ping=ping: a.activation(
                        out=expSb[ping][:, off + lo * 256: off + hi * 256],
                        in_=pt[:, lo * 256:hi * 256], func=AF.Exp, scale=0.125),
                        waits=[sc_evt, av_evt[ping]], inc="act", tag=f"exp{u}_{h}{part}")
                    if part == "A":
                        expA_evt[ping] = P.evt("act")
                        scoresA_free = expA_evt[ping]
                    else:
                        expB_evt[ping] = P.evt("act")
                        scoresB_free = expB_evt[ping]
                for kc in masked_kc:
                    P.op("pool", lambda gp, kc=kc, ping=ping: gp.tensor_mul(
                        mexp[ping][:, mslot[kc] * 256:(mslot[kc] + 1) * 256],
                        expSb[ping][:, kc * 256:(kc + 1) * 256],
                        mask_b[:, mslot[kc] * 256:(mslot[kc] + 1) * 256]),
                        waits=[expA_evt[ping] if kc < 3 else expB_evt[ping],
                               mexp_free[ping], consts_evt],
                        tag=f"mask{u}_{h}_{kc}")
                P.extra_inc("pool", "poolc")
                mk_evt[h] = P.evt("poolc")
                if h >= 2:
                    plan_av(h - 2)
                    if h % 2 == 1 and do_tr:
                        plan_tr((h - 3) // 2)
            plan_av(ATT_HEADS - 2)
            plan_av(ATT_HEADS - 1)
            if do_tr:
                plan_tr((ATT_HEADS - 2) // 2)
                plan_tr_flush()

            tr_done_prev = P.evt("pe")
            trd_done = P.evt("trd")
            aoTb_evt = trd_done
            attn_out_war = trd_done

            wo_grp_evt = {}
            do_womm = 'womm' in sections or 'wo' in sections
            if not (do_womm or 'resid' in sections or 'ln2' in sections):
                continue
            for g in (range(8) if do_womm else []):
                sl = acquire_slot()
                pt, coff, free_evt = sl
                for d in range(8):
                    P.op("pe", lambda pe, g=g, d=d, pt=pt, coff=coff: pe.matmul(
                        pt[:, coff:coff + 256],
                        wo_b[:, d * 1024 + g * 128: d * 1024 + (g + 1) * 128],
                        aoTb[:, d * 256:(d + 1) * 256],
                        start=(d == 0), stop=(d == 7)),
                        waits=([aoTb_evt, wevt["wo"], free_evt] if d == 0 else []),
                        tag=f"wo{u}_{g}_{d}")
                P.extra_inc("pe", "pe")
                grp_evt = P.evt("pe")
                P.op("dve", lambda v, g=g, pt=pt, coff=coff: v.tensor_scalar_add(
                    tTb[:, g * 256:(g + 1) * 256], pt[:, coff:coff + 256],
                    bo_s[:, g:g + 1]),
                    waits=[grp_evt, consts_evt], inc="dve", tag=f"woe{u}_{g}")
                sl[2] = P.evt("dve")
                wo_grp_evt[g] = P.evt("dve")

            if 'resid' not in sections and 'wo' not in sections:
                continue
            resid_chunk = {}
            for qr in range(4):
                for half in range(2):
                    cc = qr * 2 + half
                    if XRES_SIMPLE:
                        P.dma("sp", "xres", lambda sp, u=u, qr=qr, half=half: sp.dma_start(
                            out=xres[qr % 2][:, half * 256:(half + 1) * 256],
                            in_=xt[0:128, 0:256]),
                            waits=([xres_free[qr % 2]] if half == 0 else []),
                            tag=f"xres{u}_{qr}_{half}")
                    else:
                        P.dma("act", "xres", lambda a, u=u, qr=qr, half=half, cc=cc: a.dma_start(
                            out=xres[qr % 2][:, half * 256:(half + 1) * 256],
                            in_=xt[cc * 128:(cc + 1) * 128, u * 256:(u + 1) * 256]),
                            waits=([xres_free[qr % 2]] if half == 0 else []),
                            tag=f"xres{u}_{qr}_{half}")
                xres_ld = P.evt("xres")
                for half in range(2):
                    c = qr * 2 + half
                    if RESID_MODE >= 2:
                        P.op("dve", lambda v, u=u, qr=qr, half=half: v.tensor_sub(
                            rs1[:, 0:256], xres[qr % 2][:, half * 256:(half + 1) * 256],
                            mub[u % 2][:, :]),
                            waits=[xres_ld] + ([wo_grp_evt[c]] if wo_grp_evt else []),
                            tag=f"res{u}_{c}")
                    if RESID_MODE >= 3:
                        P.op("dve", lambda v, u=u: v.tensor_mul(
                            rs1[:, 256:512], rs1[:, 0:256], rbf[u % 2][:, :]))
                    if RESID_MODE >= 4:
                        P.op("dve", lambda v, c=c: v.tensor_add(
                            tTb[:, c * 256:(c + 1) * 256], tTb[:, c * 256:(c + 1) * 256],
                            rs1[:, 256:512]))
                        P.op("dve", lambda v, c=c: v.tensor_mul(
                            sq_glob[:, c * 256:(c + 1) * 256],
                            tTb[:, c * 256:(c + 1) * 256],
                            tTb[:, c * 256:(c + 1) * 256]), tag=f"rsq{u}_{c}")
                    if RESID_MODE < 2:
                        P.op("dve", lambda v: v.engine_nop(), waits=[xres_ld])
                    P.extra_inc("dve", "dve")
                    resid_chunk[c] = P.evt("dve")
                xres_free[qr % 2] = P.evt("dve")
            resid_evt = P.evt("dve")

            if 'ln2' not in sections and 'wo' not in sections:
                continue
            ce2 = ([resid_chunk[c] for c in range(8)]
                   if RESID_MODE >= 4 and len(resid_chunk) == 8 else None)
            mu2_evt, bc2_evt = plan_ln_stats(
                tTb, rows, [rows_free], mu2b, r2b, [resid_evt], f"L2_{u}_",
                ([p6_free, av_region_free] if ce2 is not None
                 else [resid_evt, p6_free, av_region_free]),
                chunk_evts=ce2)
            rows_free = bc2_evt
            p6_free = mu2_evt
            for c in range(8):
                hb = (c // 2) % 2
                P.op("dve", lambda v, c=c: v.tensor_sub(
                    rs1[:, 0:256], tTb[:, c * 256:(c + 1) * 256], mu2b[:, :]),
                    waits=([hb_free[hb]] if c % 2 == 0 else []), tag=f"h{u}_{c}")
                P.op("dve", lambda v: v.tensor_mul(rs1[:, 256:512], rs1[:, 0:256], r2b[:, :]))
                P.op("dve", lambda v, c=c, hb=hb: v.tensor_scalar(
                    hbuf[hb][:, (c % 2) * 256:(c % 2) * 256 + 256], rs1[:, 256:512],
                    g1_s[:, c:c + 1], b1_s[:, c:c + 1], ALU.mult, ALU.add))
                if c % 2 == 1:
                    P.extra_inc("dve", "dve")
                    hev = P.evt("dve")
                    P.dma("pool", "spl", lambda gp, u=u, c=c, hb=hb: gp.dma_start(
                        out=hts[u, :, (c - 1) * 256:(c + 1) * 256], in_=hbuf[hb][:, :]),
                        waits=[hev], tag=f"hspill{u}_{c}")
                    hb_free[hb] = P.evt("spl")

    for eng, own in (("pe", "pe"), ("act", "act"), ("dve", "dve"),
                     ("pool", "poolc"), ("sp", "ldx")):
        P.op(eng, lambda e: e.nop(), waits=[P.evt(own)], inc="p1", tag="p1_nop")
    spl_final = P.evt("spl")
    ws_final = P.evt("ws")
    xres_final = P.evt("xres")
    trd_final = P.evt("trd")
    ldx_final = P.evt("ldx")
    cld_final = P.evt("cld")
    ldw_final = P.evt("ldw")

    barrier = [("p1", 5), ws_final, spl_final, xres_final, trd_final,
               ldx_final, cld_final, ldw_final]
    Pf = [PS0[:, 0:256], PS0[:, 512:768], PS1[:, 0:256], PS1[:, 512:768]]
    Pu = [PS2[:, 256:512], PS3[:, 256:512], PS5[:, 256:512]]

    hb_evt2, gelu_evt = {}, {}
    hTb_free = [("pe", 0), ("pe", 0)]
    pu_free = [("act", 0), ("act", 0), ("act", 0)]
    pf_free = [("dve", 0)] * 4
    hq_free = [("dve", 0)] * 4
    thb8_free = ("dve", 0)
    xq_free = [("dve", 0), ("dve", 0)]
    outb_free = [("outs", 0), ("outs", 0)]
    p6b_free = ("dve", 0)
    rows3_free = ("pe", 0)
    yqf_free = [("act", 0), ("dve", 0)]
    rr_evt_d, stats_evt_d, mu_evt_d, tT2_evt_d = {}, {}, {}, {}
    yqf_evt_d, tanh_evt_d = {}, {}
    NT2B = 16

    def load_hb(T):
        P.dma("pool", "ld2g", lambda gp, T=T: gp.dma_start(
            out=hTb2[T % 2][:, :], in_=hts[T]),
            waits=barrier + [hTb_free[T % 2]], tag=f"ldhb{T}")
        hb_evt2[T] = P.evt("ld2g")

    def plan_out_loads(T):
        P.dma("sp", "ld2", lambda sp, T=T: sp.dma_start(out=yqf[:, :], in_=yts[T]),
              waits=barrier + [spl_final]
              + [(s, c + (1 if s in ("dve", "act") else 0)) for (s, c) in yqf_free],
              tag=f"ldyf{T}")
        yqf_evt_d[T] = P.evt("ld2")

    def plan_tanh(T, i):
        w = [yqf_evt_d[T]]
        if i == 0:
            w += [("pe", stats_evt_d[T][1]),
                  (thb8_free[0], thb8_free[1]
                   + (1 if thb8_free[0] == "dve" else 0))]
        P.op("act", lambda a, i=i: a.activation(
            out=thb8[:, i * 256:(i + 1) * 256], in_=yqf[:, i * 256:(i + 1) * 256],
            func=AF.Tanh, scale=0.5), waits=w, inc="act", tag=f"tanh{T}_{i}")
        tanh_evt_d[(T, i)] = P.evt("act")

    def plan_ln_bcast_T(T):
        nonlocal rows3_free, p6b_free
        mu_evt, bc_evt = plan_ln_bcast(rows3, mu3b2[T % 2], r3b2[T % 2],
                                       rr_evt_d[T], [p6b_free], f"L3b{T}_")
        mu_evt_d[T] = mu_evt
        rows3_free = bc_evt
        p6b_free = mu_evt

    def plan_out_chain(T):
        nonlocal thb8_free, yqf_free
        buf, mu, r = tT2b[T % 2], mu3b2[T % 2], r3b2[T % 2]
        mu_evt = mu_evt_d[T]
        gg_evt = ("dve", 0)
        for c in range(8):
            P.dma("sp", "ld2", lambda sp, T=T, c=c: sp.dma_start(
                out=xq[c % 2][:, :], in_=xt[c * 128:(c + 1) * 128, T * 256:(T + 1) * 256]),
                waits=barrier + [(xq_free[c % 2][0], xq_free[c % 2][1]
                                  + (1 if xq_free[c % 2][0] == "dve" else 0))],
                tag=f"ldx2{T}_{c}")
            xq_evt = P.evt("ld2")
            ob = c % 2
            P.op("dve", lambda v, c=c: v.scalar_tensor_tensor(
                rs3[:, 256:512], thb8[:, c * 256:(c + 1) * 256], 1.0,
                yqf[:, c * 256:(c + 1) * 256], ALU.add, ALU.mult),
                waits=[("act", tanh_evt_d[(T, c)][1] + 1)], inc="dve", tag=f"gg{T}_{c}")
            gg_evt = P.evt("dve")
            P.op("dve", lambda v, c=c, buf=buf, mu=mu: v.tensor_sub(
                rs3[:, 0:256], buf[:, c * 256:(c + 1) * 256], mu[:, :]),
                waits=[mu_evt], tag=f"s{T}_{c}")
            P.op("dve", lambda v, r=r, ob=ob: v.tensor_mul(outb[ob][:, :], rs3[:, 0:256], r[:, :]),
                 waits=[outb_free[ob]])
            P.op("dve", lambda v, c=c, ob=ob: v.tensor_scalar(
                rs3[:, 0:256], outb[ob][:, :], g2_s[:, c:c + 1], b2_s[:, c:c + 1],
                ALU.mult, ALU.add))
            P.op("dve", lambda v: v.tensor_mul(rs3[:, 256:512], rs3[:, 0:256], rs3[:, 256:512]))
            P.op("dve", lambda v, c=c, ob=ob: v.scalar_tensor_tensor(
                outb[ob][:, :], rs3[:, 256:512], 0.5, xq[c % 2][:, :], ALU.mult, ALU.add),
                waits=[xq_evt], inc="dve", tag=f"out{T}_{c}")
            out_evt = P.evt("dve")
            xq_free[c % 2] = out_evt
            P.op("dve", lambda v: v.engine_nop(), inc="dve", tag=f"outlag{T}_{c}")
            P.dma("act", "outs", lambda a, T=T, c=c, ob=ob: a.dma_start(
                out=outT[c * 128:(c + 1) * 128, T * 256:(T + 1) * 256], in_=outb[ob][:, :]),
                waits=[("dve", out_evt[1] + 1)], tag=f"st{T}_{c}")
            outb_free[ob] = P.evt("outs")
        thb8_free = gg_evt
        yqf_free = [tanh_evt_d[(T, 7)], gg_evt]

    n_t2 = NT2B if do_pass2 else 0
    w1c_evt, w2c_evt = {}, {}
    if n_t2:
        load_hb(0)
        for ccw in range(8):
            P.dma("sp", "w12", lambda sp, ccw=ccw: sp.dma_start(
                out=w1res[:, ccw * 4096:(ccw + 1) * 4096],
                in_=w1s[ccw * 4:(ccw + 1) * 4].rearrange("a p f -> p a f")),
                waits=barrier + [ws_evt[("w1", ccw * 4 + 3)]], tag=f"ldw1res{ccw}")
            w1c_evt[ccw] = P.evt("w12")
            P.dma("pool", "ld2g", lambda gp, ccw=ccw: gp.dma_start(
                out=w2res[:, ccw * 4096:(ccw + 1) * 4096],
                in_=w2s[ccw * 4:(ccw + 1) * 4].rearrange("a p f -> p a f")),
                waits=barrier + [ws_evt[("w2", ccw * 4 + 3)]], tag=f"ldw2res{ccw}")
            w2c_evt[ccw] = P.evt("ld2g")

    for T in range(n_t2):
        if T + 1 < n_t2:
            load_hb(T + 1)
        if T >= 1:
            plan_out_loads(T - 1)
        for c in range(32):
            pu = Pu[c % 3]
            for d in range(8):
                P.op("pe", lambda pe, c=c, d=d, pu=pu, T=T: pe.matmul(
                    pu, w1res[:, c * 1024 + d * 128: c * 1024 + (d + 1) * 128],
                    hTb2[T % 2][:, d * 256:(d + 1) * 256],
                    start=(d == 0), stop=(d == 7)),
                    waits=(barrier + [w1c_evt[c // 4], hb_evt2[T], pu_free[c % 3]]
                           if d == 0 else []),
                    tag=f"uT{T}_{c}_{d}")
            P.extra_inc("pe", "pe")
            ug_evt = P.evt("pe")
            P.op("act", lambda a, c=c, pu=pu: a.activation(
                out=uTb[:, c * 256:(c + 1) * 256], in_=pu,
                func=AF.Gelu_apprx_tanh, bias=c1_s[:, c:c + 1]),
                waits=[ug_evt], inc="act", tag=f"gelu{T}_{c}")
            gelu_evt[c] = P.evt("act")
            pu_free[c % 3] = ("act", gelu_evt[c][1] + 1)
            if T >= 1 and 12 <= c <= 26 and c % 2 == 0:
                plan_tanh(T - 1, (c - 12) // 2)
        hTb_free[T % 2] = P.evt("pe")
        act_fence(rows3, f"ugelfence{T}")
        if debug:
            P.dma("pool", "outs", lambda gp, T=T: gp.dma_start(
                out=udbg[T], in_=uTb[:, :]), waits=[P.evt("act")], tag=f"udbg{T}")
        if T >= 1:
            plan_ln_bcast_T(T - 1)
            plan_out_chain(T - 1)

        hq_evt = {}

        def plan_hq_load(dout, T=T):
            P.dma("pool", "ld2g", lambda gp, T=T, dout=dout: gp.dma_start(
                out=hq[dout % 4][:, :], in_=hts[T, :, dout * 256:(dout + 1) * 256]),
                waits=barrier + [spl_final,
                                 (hq_free[dout % 4][0], hq_free[dout % 4][1]
                                  + (1 if hq_free[dout % 4][0] == "dve" else 0))],
                tag=f"ldhq{T}_{dout}")
            hq_evt[dout] = P.evt("ld2g")

        for dout in range(4):
            plan_hq_load(dout)
        for dgrp in range(2):
            for c in range(32):
                for dl in range(4):
                    dout = dgrp * 4 + dl
                    w = []
                    if dl == 0:
                        w = [w2c_evt[c // 4], ("act", gelu_evt[c][1] + 1)]
                        if c == 0:
                            w += [(pf_free[x][0], pf_free[x][1]
                                   + (1 if pf_free[x][0] == "dve" else 0))
                                  for x in range(4)]
                    P.op("pe", lambda pe, c=c, dl=dl, dout=dout: pe.matmul(
                        Pf[dl], w2res[:, c * 1024 + dout * 128: c * 1024 + (dout + 1) * 128],
                        uTb[:, c * 256:(c + 1) * 256],
                        start=(c == 0), stop=(c == 31)),
                        waits=w, tag=f"fT{T}_{c}_{dout}")
                P.extra_inc("pe", "pe")
            P.op("pe", lambda pe: pe.nop(), inc="pe", tag=f"fdrain{T}_{dgrp}")
            fgrp_evt = P.evt("pe")
            for dl in range(4):
                dout = dgrp * 4 + dl
                P.op("dve", lambda v, dl=dl, dout=dout, T=T: v.scalar_tensor_tensor(
                    tT2b[T % 2][:, dout * 256:(dout + 1) * 256], Pf[dl],
                    c2_s[:, dout:dout + 1], hq[dout % 4][:, :],
                    ALU.add, ALU.add),
                    waits=[fgrp_evt, hq_evt[dout], consts_evt], inc="dve",
                    tag=f"fev{T}_{dout}")
                pf_free[dl] = P.evt("dve")
                hq_free[dout % 4] = pf_free[dl]
                if dout + 4 <= 7:
                    plan_hq_load(dout + 4)
            P.op("dve", lambda v: v.engine_nop(), inc="dve", tag=f"fevlag{T}_{dgrp}")
        tT2_evt = P.evt("dve")
        tT2_evt_d[T] = tT2_evt
        if debug:
            P.dma("pool", "outs", lambda gp, T=T: gp.dma_start(
                out=tdbg[T // 2, :, (T % 2) * 2048:(T % 2) * 2048 + 2048],
                in_=tT2b[T % 2][:, :]), waits=[tT2_evt], tag=f"tdbg{T}")

        rr_evt_d[T], stats_evt_d[T] = plan_ln_stats_p1(
            tT2b[T % 2], rows3, [rows3_free], [tT2_evt], f"L3_{T}_",
            [tT2_evt, p6b_free], sq_buf=sqp2)

    if n_t2:
        Tl = n_t2 - 1
        plan_out_loads(Tl)
        for i in range(8):
            plan_tanh(Tl, i)
        act_fence(rows3, "tailfence")
        plan_ln_bcast_T(Tl)
        plan_out_chain(Tl)
        P.op("dve", lambda v: v.engine_nop(), inc="dve", tag="final_dve_nop")
        out_final = P.evt("outs")
        P.op("pool", lambda gp: gp.engine_nop(), waits=[out_final], tag="final_wait")
    else:
        P.op("pool", lambda gp: gp.engine_nop(), waits=[P.evt("spl")], tag="final_wait")

    nc._dbg_handles = {"uTb": uTb, "tT2": tT2b[0], "tT2b1": tT2b[1], "mu3s": mu3b2[0], "r3s": r3b2[0]}
    nc._dbg_planner = P
    nc._dbg_mem = (SHARED_END, cursor[0], SB_LIMIT)

    engmap = {"pe": "tensor", "act": "scalar", "dve": "vector",
              "pool": "gpsimd", "sp": "sync"}
    with nc.Block() as block:
        def make_runner(engname):
            def run(eng):
                wm = {s: 0 for s in SEMS}
                for fn, waits, incs, tag in P.entries[engname]:
                    emitted_wait = False
                    for (sname, c) in waits:
                        if c > wm[sname]:
                            eng.wait_ge(sems[sname], c)
                            wm[sname] = c
                            emitted_wait = True
                    if engname == "pe" and emitted_wait:
                        eng.nop()
                    inst = fn(eng)
                    for (sname, d) in incs:
                        inst = inst.then_inc(sems[sname], d)
            return run
        block.tensor(make_runner("pe"))
        block.scalar(make_runner("act"))
        block.vector(make_runner("dve"))
        block.gpsimd(make_runner("pool"))
        block.sync(make_runner("sp"))

    return nc


_NC_CACHE = {}


def _host_inputs(inputs):
    f32 = np.float32
    bf = ml_dtypes.bfloat16
    Wq = np.ascontiguousarray(inputs["L0_attn_Wq"], f32)
    Wk = np.ascontiguousarray(inputs["L0_attn_Wk"], f32)
    Wv = np.ascontiguousarray(inputs["L0_attn_Wv"], f32)
    Wo = np.ascontiguousarray(inputs["L0_attn_Wo"], f32)
    W1 = np.ascontiguousarray(inputs["L0_attn_W1"], f32)
    W2 = np.ascontiguousarray(inputs["L0_attn_W2"], f32)
    fwt = np.ascontiguousarray(inputs["L0_ffn_weight"].T, f32)
    disha = np.asarray(inputs["L0_ffn_disha"], f32)

    w1r = np.ascontiguousarray(W1.reshape(8, 128, 32, 128).transpose(2, 0, 1, 3))
    w2r = np.ascontiguousarray(W2.reshape(32, 128, 8, 128).transpose(0, 2, 1, 3))

    dp = np.zeros((128, 1024), f32)
    dp[0:64] = disha

    pp = np.zeros((128, 128), f32)
    for p in range(128):
        pp[p, p % 64] = 1.0
    pp = pp.astype(bf)

    idn = np.eye(128, dtype=f32).astype(bf)

    masks = np.zeros((128, 4 * 256), f32)
    i = np.arange(256)[None, :]
    for slot, kc in enumerate((0, 1, 4, 5)):
        jg = kc * 128 + np.arange(128)[:, None]
        masks[:, slot * 256:(slot + 1) * 256] = (np.abs(i + 256 - jg) <= 256)
    masks = masks.astype(bf)

    def b2d(v, nch):
        return np.ascontiguousarray(np.asarray(v, f32).reshape(nch, 128).T)

    shared = {
        "wq": Wq, "wk": Wk, "wv": Wv, "wo": Wo, "fwt": fwt,
        "w1r": w1r, "w2r": w2r, "dp": dp, "pp": pp, "idn": idn, "masks": masks,
        "bvb_in": np.ascontiguousarray(
            np.broadcast_to(np.asarray(inputs["L0_attn_bv"], f32), (128, 1024))),
        "bq2": b2d(inputs["L0_attn_bq"], 8), "bk2": b2d(inputs["L0_attn_bk"], 8),
        "bo2": b2d(inputs["L0_attn_bo"], 8), "g12": b2d(inputs["L0_attn_g1"], 8),
        "b12": b2d(inputs["L0_attn_b1"], 8), "g22": b2d(inputs["L0_attn_g2"], 8),
        "b22": b2d(inputs["L0_attn_b2"], 8), "c22": b2d(inputs["L0_attn_c2"], 8),
        "c12": b2d(inputs["L0_attn_c1"], 32),
        "ones128": np.ones((128, 1), f32),
        "ones1x128": np.ones((1, 128), f32),
    }
    return shared


def kernel(**inputs):
    x = np.asarray(inputs["L0_attn_x"], np.float32)
    shared = _host_inputs(inputs)
    if "nc" not in _NC_CACHE:
        _NC_CACHE["nc"] = build_program()
    nc = _NC_CACHE["nc"]
    in_maps = []
    for c in range(8):
        m = dict(shared)
        m["xt"] = np.ascontiguousarray(x[c].T)
        in_maps.append(m)
    res = run_bass_kernel_spmd(nc, in_maps, core_ids=list(range(8)))
    out = np.empty((8, 4096, 1024), np.float32)
    for c in range(8):
        out[c] = np.asarray(res.results[c]["outT"]).T
    return out


# version 33
# speedup vs baseline: 1.0658x; 1.0658x over previous
import numpy as np
import ml_dtypes

import concourse.bass as bass
import concourse.mybir as mybir
from concourse.bass_utils import run_bass_kernel_spmd

F32 = mybir.dt.float32
BF16 = mybir.dt.bfloat16
AF = mybir.ActivationFunctionType
ALU = mybir.AluOpType

D = 1024
S = 4096
NT = 16
NT2 = 8
TR_LIMIT = 10**9
TR_NOEVAC = False
TR_ALTSRC = False
ATT_HEADS = 16
NO_NORM = False
RESID_MODE = 4
XRES_SIMPLE = False
TR_FAKESRC = False
EPS = 1e-5

ENGINES = ("pe", "act", "dve", "pool", "sp")
SEMS = ("pe", "act", "dve", "poolc", "ldx", "ldw", "cld", "spl", "ws", "w12",
        "ld2", "ld2g", "xres", "outs", "trd", "p1")
DMA_SEMS = {"ldx", "ldw", "cld", "spl", "ws", "w12", "ld2", "ld2g", "xres", "outs", "trd"}


class Planner:
    def __init__(self):
        self.entries = {e: [] for e in ENGINES}
        self.cnt = {s: 0 for s in SEMS}

    def op(self, eng, fn, waits=(), inc=None, tag=""):
        waits = [(s, c) for (s, c) in waits if c > 0]
        incs = []
        if inc is not None:
            d = 16 if inc in DMA_SEMS else 1
            self.cnt[inc] += d
            incs.append((inc, d))
        self.entries[eng].append([fn, waits, incs, tag])

    def dma(self, eng, sem, fn, waits=(), tag=""):
        self.op(eng, fn, waits, inc=sem, tag=tag)

    def evt(self, sem):
        return (sem, self.cnt[sem])

    def extra_inc(self, eng, sem):
        d = 16 if sem in DMA_SEMS else 1
        self.cnt[sem] += d
        self.entries[eng][-1][2].append((sem, d))


def build_program(debug=False, p1_tiles=NT, do_pass2=True, sections=('proj','attn','wo')):
    nc = bass.Bass("TRN2", target_bir_lowering=False, detect_race_conditions=False)

    dr = {}
    def din(name, shape, dt=F32):
        dr[name] = nc.dram_tensor(name, list(shape), dt, kind="ExternalInput").ap()
        return dr[name]

    xt = din("xt", (D, S))
    din("wq", (D, D)); din("wk", (D, D)); din("wv", (D, D)); din("wo", (D, D))
    din("fwt", (D, D))
    w1r = din("w1r", (32, 8, 128, 128))
    w2r = din("w2r", (32, 8, 128, 128))
    din("dp", (128, D))
    din("pp", (128, 128), BF16)
    din("idn", (128, 128), BF16)
    din("masks", (128, 4 * 256), BF16)
    din("bvb_in", (128, D))
    for nm in ("bq2", "bk2", "bo2", "g12", "b12", "g22", "b22", "c22"):
        din(nm, (128, 8))
    din("c12", (128, 32))
    din("ones128", (128, 1))
    din("ones1x128", (1, 128))

    outT = nc.dram_tensor("outT", [D, S], F32, kind="ExternalOutput").ap()
    sk = "ExternalOutput" if debug else "Internal"
    hts = nc.dram_tensor("hts", [NT, 128, 2048], F32, kind=sk).ap()
    yts = nc.dram_tensor("yts", [NT, 128, 2048], F32, kind=sk).ap()
    tdbg = nc.dram_tensor("tdbg", [NT2, 128, 4096], F32, kind=sk).ap() if debug else None
    udbg = nc.dram_tensor("udbg", [16, 128, 8192], BF16, kind=sk).ap() if debug else None
    hdbg = nc.dram_tensor("hdbg", [NT2, 128, 4096], F32, kind=sk).ap() if debug else None
    mdbg = nc.dram_tensor("mdbg", [NT2, 128, 1024], F32, kind=sk).ap() if debug else None
    w1s = nc.dram_tensor("w1s", [32, 128, 1024], BF16, kind="Internal").ap()
    w2s = nc.dram_tensor("w2s", [32, 128, 1024], BF16, kind="Internal").ap()

    P = Planner()
    sems = {s: nc.alloc_semaphore(name="s_" + s) for s in SEMS}

    SB_LIMIT = 229370
    cursor = [18432]
    def salloc(name, shape, dt, region_start=None):
        if region_start is not None:
            cursor[0] = region_start
        nbytes = shape[1] * mybir.dt.size(dt)
        off = (cursor[0] + 31) // 32 * 32
        cursor[0] = off + nbytes
        assert cursor[0] <= SB_LIMIT, f"SBUF overflow at {name}: {cursor[0]}"
        return nc.alloc_sbuf_tensor_at(name, list(shape), dt, offset=off)

    bq_s = salloc("bq_s", [128, 8], F32)
    bk_s = salloc("bk_s", [128, 8], F32)
    bo_s = salloc("bo_s", [128, 8], F32)
    g1_s = salloc("g1_s", [128, 8], F32)
    b1_s = salloc("b1_s", [128, 8], F32)
    g2_s = salloc("g2_s", [128, 8], F32)
    b2_s = salloc("b2_s", [128, 8], F32)
    c2_s = salloc("c2_s", [128, 8], F32)
    c1_s = salloc("c1_s", [128, 32], F32)
    on128 = salloc("on128", [128, 1], F32)
    on1x = salloc("on1x", [1, 128], F32)
    cst = salloc("cst", [128, 4], F32)
    idn_b = salloc("idn_b", [128, 128], BF16)
    pp_b = salloc("pp_b", [128, 128], BF16)
    mask_b = salloc("mask_b", [128, 1024], BF16)
    SHARED_END = cursor[0]

    wq_b = salloc("wq_b", [128, 8192], BF16)
    wk_b = salloc("wk_b", [128, 8192], BF16)
    wv_b = salloc("wv_b", [128, 8192], BF16)
    wo_b = salloc("wo_b", [128, 8192], BF16)
    fw_b = salloc("fw_b", [128, 8192], BF16)
    dp_b = salloc("dp_b", [128, 1024], BF16)
    bvb = salloc("bvb", [128, 1024], BF16)
    wstg = [salloc("wstg0", [128, 1024], BF16)] * 2
    xT = [salloc(f"xT{i}", [128, 2048], F32) for i in range(2)]
    xres = [salloc(f"xres{i}", [128, 512], F32) for i in range(2)]
    sq_glob = salloc("sq", [128, 2048], F32)
    rows = salloc("rows", [1, 1280], F32)
    mub = [salloc(f"mub{i}", [128, 256], BF16) for i in range(2)]
    rbf = [salloc(f"rbf{i}", [128, 256], F32) for i in range(2)]
    mu2b = salloc("mu2b", [128, 256], BF16)
    r2b = salloc("r2b", [128, 256], F32)
    xnTb = salloc("xnTb", [128, 2048], BF16)
    qTb = [salloc(f"qTb{i}", [128, 2048], BF16) for i in range(2)]
    kTb = [salloc(f"kTb{i}", [128, 2048], BF16) for i in range(3)]
    vaug = [salloc(f"vaug{i}", [128, 2080], BF16) for i in range(3)]
    expSb = [salloc(f"expSb{i}", [128, 1536], BF16) for i in range(3)]
    mexp = [salloc(f"mexp{i}", [128, 1024], BF16) for i in range(3)]
    rcpt = salloc("rcpt", [128, 32], F32)
    attn_out = salloc("attn_out", [128, 2048], BF16)
    aoTb = salloc("aoTb", [128, 2048], BF16)
    tTb = salloc("tTb", [128, 2048], F32)
    rs1 = salloc("rs1", [128, 512], F32)
    hbuf = [salloc(f"hbuf{i}", [128, 512], F32) for i in range(2)]
    ybuf = [salloc(f"ybuf{i}", [128, 512], F32) for i in range(2)]
    pooledTb = salloc("pooledTb", [128, 256], BF16)
    av_sb = [salloc(f"av_sb{i}", [128, 130], F32) for i in range(2)]

    w1res = salloc("w1res", [128, 32768], BF16, region_start=SHARED_END)
    w2res = salloc("w2res", [128, 32768], BF16)
    uTb = salloc("uTb", [128, 8192], BF16)
    hTb2 = [salloc(f"hTb2_{i}", [128, 2048], BF16) for i in range(2)]
    hq = [salloc(f"hq{i}", [128, 256], F32) for i in range(4)]
    tT2b = [salloc(f"tT2_{i}", [128, 2048], F32) for i in range(2)]
    sqp2 = salloc("sqp2", [128, 2048], F32)
    rows3 = salloc("rows3", [1, 1280], F32)
    mu3b2 = [salloc(f"mu3b{i}", [128, 256], BF16) for i in range(2)]
    r3b2 = [salloc(f"r3b{i}", [128, 256], F32) for i in range(2)]
    yqf = salloc("yqf", [128, 2048], F32)
    xq = [salloc(f"xq{i}", [128, 256], F32) for i in range(2)]
    thb8 = sqp2
    rs3 = salloc("rs3", [128, 512], F32)
    outb = [salloc(f"outb{i}", [128, 256], F32) for i in range(2)]

    PS5 = nc.alloc_psum_tensor("PS5", [128, 512], F32)
    PS0 = nc.alloc_psum_tensor("PS0", [128, 1024], F32)
    PS1 = nc.alloc_psum_tensor("PS1", [128, 1024], F32)
    PS2 = nc.alloc_psum_tensor("PS2", [128, 512], F32)
    PS3 = nc.alloc_psum_tensor("PS3", [128, 512], F32)
    PS4 = nc.alloc_psum_tensor("PS4", [128, 512], F32)
    PA, PB, P4, P5, P6, P7t = PS0, PS1, PS2, PS3, PS4, PS5

    for nm, tile in (("bq2", bq_s), ("bk2", bk_s), ("bo2", bo_s), ("g12", g1_s),
                     ("b12", b1_s), ("g22", g2_s), ("b22", b2_s), ("c22", c2_s),
                     ("c12", c1_s), ("ones128", on128), ("ones1x128", on1x),
                     ("idn", idn_b), ("pp", pp_b), ("masks", mask_b)):
        P.dma("sp", "ldw", lambda sp, a=dr[nm], t=tile: sp.dma_start(out=t[:, :], in_=a),
              tag="ldc_" + nm)
    consts_evt = P.evt("ldw")

    wevt = {}
    for name, dst in (("wq", wq_b), ("wk", wk_b), ("wv", wv_b), ("fw", fw_b), ("wo", wo_b)):
        src = dr[{"fw": "fwt"}.get(name, name)]
        for d in range(8):
            P.dma("pool", "cld", lambda gp, s=src, t=dst, d=d: gp.dma_start(
                out=t[:, d * 1024:(d + 1) * 1024], in_=s[d * 128:(d + 1) * 128, :]),
                tag=f"cast_{name}{d}")
        wevt[name] = P.evt("cld")
    P.dma("pool", "cld", lambda gp: gp.dma_start(out=dp_b[:, :], in_=dr["dp"]), tag="cast_dp")
    wevt["dp"] = P.evt("cld")
    P.dma("pool", "cld", lambda gp: gp.dma_start(out=bvb[:, :], in_=dr["bvb_in"]), tag="cast_bv")
    wevt["bv"] = P.evt("cld")

    for b in range(3):
        for i in range(2):
            for h in range(16):
                col = i * 1040 + h * 65 + 64
                P.op("dve", lambda v, b=b, c=col: v.memset(vaug[b][:, c:c + 1], 1.0),
                     tag="ones_v")
    P.op("dve", lambda v: v.memset(cst[:, 0:1], EPS), inc="dve", tag="eps")
    eps_evt = P.evt("dve")

    def load_xT(t, clo=0, chi=8):
        def fn(sp, t=t, clo=clo, chi=chi):
            src = xt[clo * 128:chi * 128,
                     t * 256:(t + 1) * 256].rearrange("(c p) n -> p c n", p=128)
            dst = xT[t % 2][:, clo * 256:chi * 256].rearrange(
                "p (c n) -> p c n", n=256)
            return sp.dma_start(out=dst, in_=src)
        return fn
    ldx_evt = {}
    P.dma("sp", "ldx", load_xT(0, 0, 4), tag="ldx0a")
    P.dma("act", "ldx", load_xT(0, 4, 8), tag="ldx0b")
    ldx_evt[0] = P.evt("ldx")

    precast_jobs = [("w1", c) for c in range(32)] + [("w2", c) for c in range(32)]
    ws_evt = {}
    stg_free = [("ws", 0), ("ws", 0)]

    def plan_precast(job_idx):
        which, c = precast_jobs[job_idx]
        buf = job_idx % 2
        src = w1r if which == "w1" else w2r
        dst = w1s if which == "w1" else w2s
        def cast_fn(gp, src=src, c=c, buf=buf):
            return gp.dma_start(out=wstg[buf][:, :].rearrange("p (a f) -> p a f", f=128),
                                in_=src[c].rearrange("a p f -> p a f"))
        P.dma("pool", "cld", cast_fn, waits=[stg_free[buf]], tag=f"pcc_{which}{c}")
        cast_evt = P.evt("cld")
        def store_fn(sp, dst=dst, c=c, buf=buf):
            return sp.dma_start(out=dst[c], in_=wstg[buf][:, :])
        P.dma("sp", "ws", store_fn, waits=[cast_evt], tag=f"pcs_{which}{c}")
        ws_evt[(which, c)] = P.evt("ws")
        stg_free[0] = ws_evt[(which, c)]
        stg_free[1] = ws_evt[(which, c)]

    xn_evt, mu_evt, qT_evt, kT_evt, v_evt = {}, {}, {}, {}, {}
    p6_free = ("dve", 0)
    rows_free = ("pe", 0)
    hb_free = [("spl", 0), ("spl", 0)]
    yb_free = [("spl", 0), ("spl", 0)]
    xres_free = [("dve", 0), ("dve", 0)]
    slots = [[P4, 0, ("act", 0)], [P5, 0, ("act", 0)]]
    slot_rr = [0]
    def acquire_slot():
        sl = slots[slot_rr[0] % 2]
        slot_rr[0] += 1
        return sl
    expA_evt = [("act", 0), ("act", 0), ("act", 0)]
    expB_evt = [("act", 0), ("act", 0), ("act", 0)]
    av_evt = [("pe", 0), ("pe", 0), ("pe", 0)]
    mexp_free = [("pe", 0), ("pe", 0), ("pe", 0)]
    rcp_free = {}
    tr_done_prev = ("pe", 0)
    attn_out_war = ("trd", 0)
    aoTb_war = ("pe", 0)
    scoresA_free = ("act", 0)
    scoresB_free = ("act", 0)
    av_region_free = ("act", 0)
    p7_free = [("dve", 0), ("dve", 0), ("dve", 0)]

    def act_fence(rows_t, tag):
        P.op("act", lambda a, rows_t=rows_t: a.activation(
            out=rows_t[0:1, 1279:1280] if rows_t.shape[1] == 1280 else rows_t[0:1, 2047:2048],
            in_=cst[0:1, 0:1], func=AF.Identity), inc="act", tag=tag)
        return P.evt("act")

    def plan_ln_stats_p1(src_tile, rows_t, rows_waits, sq_evt_out,
                         tagp, extra_stats_waits, sq_buf=None, chunk_evts=None):
        sq = sq_buf if sq_buf is not None else sq_glob
        """LN statistics over 8 chunks of (128,256) from src_tile -> broadcast
        mu into mu_dst (bf16), rstd into r_dst (f32) via P6. Returns evt of the
        mu/r evac (dve). With chunk_evts, the caller has already produced both
        src chunk c and its square in sq chunk c by chunk_evts[c]; the sums
        then pipeline per chunk behind the producer instead of waiting for the
        whole tile."""
        if chunk_evts is None:
            for c in range(8):
                P.op("dve", lambda v, c=c: v.tensor_mul(
                    sq[:, c * 256:(c + 1) * 256], src_tile[:, c * 256:(c + 1) * 256],
                    src_tile[:, c * 256:(c + 1) * 256]),
                    waits=(sq_evt_out if c == 0 else []), tag=f"{tagp}sq{c}")
            P.extra_inc("dve", "dve")
            sqr_evt = P.evt("dve")
        for c in range(8):
            w = (extra_stats_waits if c == 0 else [])
            if chunk_evts is not None:
                w = w + [chunk_evts[c]]
            P.op("pe", lambda pe, c=c: pe.matmul(
                P6[0:1, 0:256], on128[:, 0:1], src_tile[:, c * 256:(c + 1) * 256],
                start=(c == 0), stop=(c == 7)),
                waits=w, tag=f"{tagp}sums{c}")
        P.extra_inc("pe", "pe")
        for c in range(8):
            w = []
            if chunk_evts is None and c == 0:
                w = [sqr_evt]
            P.op("pe", lambda pe, c=c: pe.matmul(
                P6[0:1, 256:512], on128[:, 0:1], sq[:, c * 256:(c + 1) * 256],
                start=(c == 0), stop=(c == 7)),
                waits=w, tag=f"{tagp}sumsq{c}")
        P.extra_inc("pe", "pe")
        stats_evt = P.evt("pe")
        P.op("dve", lambda v: v.tensor_copy(rows_t[0:1, 512:1024], P6[0:1, 0:512]),
             waits=[stats_evt] + rows_waits, tag=f"{tagp}rowcp")
        P.op("dve", lambda v: v.tensor_scalar_mul(rows_t[0:1, 0:256], rows_t[0:1, 512:768], 1.0 / D))
        P.op("dve", lambda v: v.tensor_scalar_mul(rows_t[0:1, 1024:1280], rows_t[0:1, 768:1024], 1.0 / D))
        P.op("dve", lambda v: v.tensor_mul(rows_t[0:1, 768:1024], rows_t[0:1, 0:256], rows_t[0:1, 0:256]))
        P.op("dve", lambda v: v.tensor_sub(rows_t[0:1, 512:768], rows_t[0:1, 1024:1280], rows_t[0:1, 768:1024]))
        P.op("dve", lambda v: v.tensor_scalar_add(rows_t[0:1, 768:1024], rows_t[0:1, 512:768], EPS),
             inc="dve")
        var_evt = P.evt("dve")
        P.op("act", lambda a: a.activation(out=rows_t[0:1, 1024:1280], in_=rows_t[0:1, 768:1024],
                                           func=AF.Ln),
             waits=[var_evt], tag=f"{tagp}ln")
        P.op("act", lambda a: a.activation(out=rows_t[0:1, 256:512], in_=rows_t[0:1, 1024:1280],
                                           func=AF.Exp, scale=-0.5), inc="act")
        rr_evt = act_fence(rows_t, f"{tagp}fence")
        return rr_evt, stats_evt

    def plan_ln_stats(src_tile, rows_t, rows_waits, mu_dst, r_dst, sq_evt_out,
                      tagp, extra_stats_waits, sq_buf=None, chunk_evts=None):
        rr_evt, _ = plan_ln_stats_p1(src_tile, rows_t, rows_waits, sq_evt_out,
                                     tagp, extra_stats_waits, sq_buf, chunk_evts)
        return plan_ln_bcast(rows_t, mu_dst, r_dst, rr_evt, [], tagp)

    def plan_ln_bcast(rows_t, mu_dst, r_dst, rr_evt, bcast_extra, tagp):
        P.op("pe", lambda pe, rows_t=rows_t: pe.matmul(
            P6[:, 0:512], on1x[0:1, :], rows_t[0:1, 0:512], start=True, stop=True),
             waits=[rr_evt] + bcast_extra, inc="pe", tag=f"{tagp}bcast")
        bc_evt = P.evt("pe")
        P.op("dve", lambda v, mu_dst=mu_dst: v.tensor_copy(mu_dst[:, :], P6[:, 0:256]),
             waits=[bc_evt], tag=f"{tagp}mue")
        P.op("dve", lambda v, r_dst=r_dst: v.tensor_copy(r_dst[:, :], P6[:, 256:512]),
             inc="dve")
        return P.evt("dve"), bc_evt

    for s in range(p1_tiles + 1):
        t, u = s, s - 1
        if t >= p1_tiles:
            t = NT

        for j in range(4):
            ji = s * 4 + j
            if ji < len(precast_jobs):
                plan_precast(ji)

        if t < NT:
            if t + 1 < NT:
                P.dma("sp", "ldx", load_xT(t + 1),
                      waits=[xn_evt.get(t - 1, ("dve", 0))], tag=f"ldx{t+1}")
                ldx_evt[t + 1] = P.evt("ldx")

            mu_evt[t], bc1_evt = plan_ln_stats(
                xT[t % 2], rows, [rows_free], mub[t % 2], rbf[t % 2],
                [ldx_evt[t]], f"L1_{t}_",
                [ldx_evt[t], consts_evt, p6_free])
            rows_free = bc1_evt
            p6_free = mu_evt[t]
            for c in range(8):
                P.op("dve", lambda v, t=t, c=c: v.tensor_sub(
                    sq_glob[:, c * 256:(c + 1) * 256], xT[t % 2][:, c * 256:(c + 1) * 256],
                    mub[t % 2][:, :]), tag=f"xns{t}_{c}")
                P.op("dve", lambda v, t=t, c=c: v.tensor_mul(
                    xnTb[:, c * 256:(c + 1) * 256], sq_glob[:, c * 256:(c + 1) * 256],
                    rbf[t % 2][:, :]))
            P.extra_inc("dve", "dve")
            xn_evt[t] = P.evt("dve")

            for name, wb, bias_s, dst in (("q", wq_b, bq_s, qTb[t % 2]),
                                          ("k", wk_b, bk_s, kTb[t % 3])):
                for g in range(8):
                    sl = acquire_slot()
                    pt, coff, free_evt = sl
                    for d in range(8):
                        P.op("pe", lambda pe, wb=wb, g=g, d=d, pt=pt, coff=coff: pe.matmul(
                            pt[:, coff:coff + 256],
                            wb[:, d * 1024 + g * 128: d * 1024 + (g + 1) * 128],
                            xnTb[:, d * 256:(d + 1) * 256],
                            start=(d == 0), stop=(d == 7)),
                            waits=([xn_evt[t], wevt["wq" if name == "q" else "wk"],
                                    free_evt, tr_done_prev] if d == 0 else []),
                            tag=f"{name}T{t}_{g}_{d}")
                    P.extra_inc("pe", "pe")
                    grp_evt = P.evt("pe")
                    P.op("dve", lambda v, dst=dst, g=g, pt=pt, coff=coff, bs=bias_s: v.tensor_scalar_add(
                        dst[:, g * 256:(g + 1) * 256], pt[:, coff:coff + 256],
                        bs[:, g:g + 1]),
                        waits=[grp_evt, consts_evt], inc="dve", tag=f"{name}Te{t}_{g}")
                    sl[2] = P.evt("dve")
                if name == "q":
                    qT_evt[t] = P.evt("dve")
                else:
                    kT_evt[t] = P.evt("dve")

            for i in range(2):
                for j in range(2):
                    fb = (i * 2 + j) % 2
                    pt = (P4, P5)[fb]
                    for d in range(8):
                        P.op("pe", lambda pe, i=i, j=j, d=d, pt=pt: pe.matmul(
                            pt[:, 0:512],
                            xnTb[:, d * 256 + i * 128: d * 256 + i * 128 + 128],
                            wv_b[:, d * 1024 + j * 512: d * 1024 + (j + 1) * 512],
                            start=(d == 0), stop=(d == 7)),
                            waits=([xn_evt[t], wevt["wv"], slots[fb][2]] if d == 0 else []),
                            tag=f"v{t}_{i}{j}_{d}")
                    P.extra_inc("pe", "pe")
                    grp_evt = P.evt("pe")
                    def vevac(v, t=t, i=i, j=j, pt=pt):
                        outap = vaug[t % 3][:, i * 1040 + j * 520: i * 1040 + (j + 1) * 520] \
                            .rearrange("p (h c) -> p h c", c=65)[:, :, 0:64]
                        in0 = pt[:, 0:512].rearrange("p (h c) -> p h c", c=64)
                        in1 = bvb[:, j * 512:(j + 1) * 512].rearrange("p (h c) -> p h c", c=64)
                        return v.tensor_tensor(outap, in0, in1, ALU.add)
                    P.op("dve", vevac, waits=[grp_evt, wevt["bv"], tr_done_prev],
                         inc="dve", tag=f"vev{t}_{i}{j}")
                    slots[fb][2] = P.evt("dve")
            v_evt[t] = P.evt("dve")

            for d in range(8):
                P.op("pe", lambda pe, d=d: pe.matmul(
                    P6[:, 0:256], pp_b[:, :], xnTb[:, d * 256:(d + 1) * 256],
                    start=(d == 0), stop=(d == 7)),
                    waits=([xn_evt[t], consts_evt, p6_free] if d == 0 else []),
                    tag=f"pool{t}_{d}")
            P.extra_inc("pe", "pe")
            pool_evt = P.evt("pe")
            P.op("dve", lambda v: v.tensor_copy(pooledTb[:, :], P6[:, 0:256]),
                 waits=[pool_evt], inc="dve", tag=f"poole{t}")
            p6_free = P.evt("dve")
            pooled_evt = p6_free

            for g in range(8):
                sl = acquire_slot()
                pt, coff, free_evt = sl
                for d in range(8):
                    P.op("pe", lambda pe, g=g, d=d, pt=pt, coff=coff: pe.matmul(
                        pt[:, coff:coff + 256],
                        fw_b[:, d * 1024 + g * 128: d * 1024 + (g + 1) * 128],
                        xnTb[:, d * 256:(d + 1) * 256],
                        start=(d == 0), stop=False),
                        waits=([xn_evt[t], wevt["fw"], wevt["dp"], pooled_evt, free_evt]
                               if d == 0 else []),
                        tag=f"y{t}_{g}_{d}")
                P.op("pe", lambda pe, g=g, pt=pt, coff=coff: pe.matmul(
                    pt[:, coff:coff + 256], dp_b[:, g * 128:(g + 1) * 128],
                    pooledTb[:, :], start=False, stop=True), inc="pe", tag=f"yd{t}_{g}")
                grp_evt = P.evt("pe")
                yb = (g // 2) % 2
                P.op("dve", lambda v, g=g, pt=pt, coff=coff, yb=yb: v.tensor_copy(
                    ybuf[yb][:, (g % 2) * 256:(g % 2) * 256 + 256], pt[:, coff:coff + 256]),
                    waits=[grp_evt] + ([yb_free[yb]] if g % 2 == 0 else []),
                    inc="dve", tag=f"yev{t}_{g}")
                sl[2] = P.evt("dve")
                if g % 2 == 1:
                    yev = P.evt("dve")
                    P.dma("pool", "spl", lambda gp, t=t, g=g, yb=yb: gp.dma_start(
                        out=yts[t, :, (g - 1) * 256:(g + 1) * 256], in_=ybuf[yb][:, :]),
                        waits=[yev], tag=f"yspill{t}_{g}")
                    yb_free[yb] = P.evt("spl")

        if u >= 0 and 'attn' in sections:
            valid_kc = [kc for kc in range(6) if 0 <= (u - 1 + kc // 2) <= 15]
            masked_kc = [kc for kc in valid_kc if kc in (0, 1, 4, 5)]
            mslot = {0: 0, 1: 1, 4: 2, 5: 3}
            mk_evt = {}

            def plan_av(h):
                nonlocal av_region_free
                ping = h % 3
                for q2 in range(2):
                    n = len(valid_kc)
                    for ii, kc in enumerate(valid_kc):
                        b = (u - 1 + kc // 2) % 3
                        if kc in masked_kc:
                            lhsT = mexp[ping][:, mslot[kc] * 256 + q2 * 128:
                                              mslot[kc] * 256 + q2 * 128 + 128]
                        else:
                            lhsT = expSb[ping][:, kc * 256 + q2 * 128:
                                               kc * 256 + q2 * 128 + 128]
                        rhs = vaug[b][:, (kc % 2) * 1040 + h * 65:(kc % 2) * 1040 + h * 65 + 65]
                        P.op("pe", lambda pe, lhsT=lhsT, rhs=rhs, q2=q2, ii=ii, n=n: pe.matmul(
                            P6[:, 256 + q2 * 65:256 + q2 * 65 + 65], lhsT, rhs,
                            start=(ii == 0), stop=(ii == n - 1)),
                            waits=([mk_evt[h], expB_evt[ping], av_region_free, p6_free,
                                    v_evt[min(u + 1, p1_tiles - 1)]] if (ii == 0 and q2 == 0) else []),
                            tag=f"av{u}_{h}_{q2}_{kc}")
                    P.extra_inc("pe", "pe")
                avh_evt = P.evt("pe")
                av_evt[ping] = avh_evt
                mexp_free[ping] = avh_evt
                sb = av_sb[h % 2]
                P.op("dve", lambda v, sb=sb: v.tensor_copy(sb[:, :], P6[:, 256:386]),
                     waits=[avh_evt], inc="dve", tag=f"avev{u}_{h}")
                evac_evt = P.evt("dve")
                av_region_free = ("dve", evac_evt[1] + 1)
                for q2 in range(2):
                    col = (h % 3) * 2 + q2
                    P.op("dve", lambda v, q2=q2, col=col, sb=sb: v.reciprocal(
                        rcpt[:, col * 4:col * 4 + 1], sb[:, q2 * 65 + 64:q2 * 65 + 65]),
                        waits=[rcp_free.get(col, ("dve", 0))], inc="dve",
                        tag=f"rcp{u}_{h}_{q2}")
                    rc_evt = P.evt("dve")
                    P.op("dve", lambda v, h=h, q2=q2, col=col, sb=sb: v.tensor_scalar(
                        attn_out[:, q2 * 1024 + h * 64: q2 * 1024 + (h + 1) * 64],
                        sb[:, q2 * 65:q2 * 65 + 64],
                        rcpt[:, col * 4:col * 4 + 1], None, ALU.mult),
                        waits=[rc_evt, attn_out_war], inc="dve", tag=f"nrm{u}_{h}_{q2}")
                    norm_evt[(h, q2)] = P.evt("dve")
                    rcp_free[col] = norm_evt[(h, q2)]

            def plan_tr(c):
                for q2 in range(2):
                    if c * 2 + q2 >= TR_LIMIT:
                        continue
                    P.dma("sp", "trd", lambda sp, c=c, q2=q2: sp.dma_start_transpose(
                        aoTb[:, c * 256 + q2 * 128: c * 256 + q2 * 128 + 128],
                        attn_out[:, q2 * 1024 + c * 128: q2 * 1024 + (c + 1) * 128]),
                        waits=[norm_evt[(2 * c + 1, q2)], norm_evt[(2 * c, q2)],
                               aoTb_war],
                        tag=f"tr{u}_{c}_{q2}")

            def plan_tr_flush():
                pass

            do_tr = 'tr' in sections or 'wo' in sections
            norm_evt = {}
            for h in range(ATT_HEADS):
                ping = h % 3
                for part in ("A", "B"):
                    kcs = [kc for kc in valid_kc if (kc < 3) == (part == "A")]
                    pt = PA if part == "A" else PB
                    free_evt = scoresA_free if part == "A" else scoresB_free
                    for kc in kcs:
                        hp = (h % 2) * 64
                        b = (u - 1 + kc // 2) % 3
                        lhsT = kTb[b][hp:hp + 64,
                                      (h // 2) * 256 + (kc % 2) * 128:
                                      (h // 2) * 256 + (kc % 2) * 128 + 128]
                        rhs = qTb[u % 2][hp:hp + 64, (h // 2) * 256:(h // 2) * 256 + 256]
                        P.op("pe", lambda pe, lhsT=lhsT, rhs=rhs, kc=kc, pt=pt: pe.matmul(
                            pt[:, (kc % 3) * 256:(kc % 3) * 256 + 256], lhsT, rhs,
                            start=True, stop=True),
                            waits=([qT_evt[u], kT_evt[min(u + 1, p1_tiles - 1)], free_evt]
                                   if kc == kcs[0] else []),
                            tag=f"sc{u}_{h}{part}_{kc}")
                    P.extra_inc("pe", "pe")
                    sc_evt = P.evt("pe")
                    lo = min(kc % 3 for kc in kcs)
                    hi = max(kc % 3 for kc in kcs) + 1
                    off = 0 if part == "A" else 768
                    P.op("act", lambda a, pt=pt, lo=lo, hi=hi, off=off, ping=ping: a.activation(
                        out=expSb[ping][:, off + lo * 256: off + hi * 256],
                        in_=pt[:, lo * 256:hi * 256], func=AF.Exp, scale=0.125),
                        waits=[sc_evt, av_evt[ping]], inc="act", tag=f"exp{u}_{h}{part}")
                    if part == "A":
                        expA_evt[ping] = P.evt("act")
                        scoresA_free = expA_evt[ping]
                    else:
                        expB_evt[ping] = P.evt("act")
                        scoresB_free = expB_evt[ping]
                for kc in masked_kc:
                    P.op("pool", lambda gp, kc=kc, ping=ping: gp.tensor_mul(
                        mexp[ping][:, mslot[kc] * 256:(mslot[kc] + 1) * 256],
                        expSb[ping][:, kc * 256:(kc + 1) * 256],
                        mask_b[:, mslot[kc] * 256:(mslot[kc] + 1) * 256]),
                        waits=[expA_evt[ping] if kc < 3 else expB_evt[ping],
                               mexp_free[ping], consts_evt],
                        tag=f"mask{u}_{h}_{kc}")
                P.extra_inc("pool", "poolc")
                mk_evt[h] = P.evt("poolc")
                if h >= 2:
                    plan_av(h - 2)
                    if h % 2 == 1 and do_tr:
                        plan_tr((h - 3) // 2)
            plan_av(ATT_HEADS - 2)
            plan_av(ATT_HEADS - 1)
            if do_tr:
                plan_tr((ATT_HEADS - 2) // 2)
                plan_tr_flush()

            tr_done_prev = P.evt("pe")
            trd_done = P.evt("trd")
            aoTb_evt = trd_done
            attn_out_war = trd_done

            wo_grp_evt = {}
            do_womm = 'womm' in sections or 'wo' in sections
            if not (do_womm or 'resid' in sections or 'ln2' in sections):
                continue
            for g in (range(8) if do_womm else []):
                sl = acquire_slot()
                pt, coff, free_evt = sl
                for d in range(8):
                    P.op("pe", lambda pe, g=g, d=d, pt=pt, coff=coff: pe.matmul(
                        pt[:, coff:coff + 256],
                        wo_b[:, d * 1024 + g * 128: d * 1024 + (g + 1) * 128],
                        aoTb[:, d * 256:(d + 1) * 256],
                        start=(d == 0), stop=(d == 7)),
                        waits=([aoTb_evt, wevt["wo"], free_evt] if d == 0 else []),
                        tag=f"wo{u}_{g}_{d}")
                P.extra_inc("pe", "pe")
                grp_evt = P.evt("pe")
                P.op("dve", lambda v, g=g, pt=pt, coff=coff: v.tensor_scalar_add(
                    tTb[:, g * 256:(g + 1) * 256], pt[:, coff:coff + 256],
                    bo_s[:, g:g + 1]),
                    waits=[grp_evt, consts_evt], inc="dve", tag=f"woe{u}_{g}")
                sl[2] = P.evt("dve")
                wo_grp_evt[g] = P.evt("dve")

            if 'resid' not in sections and 'wo' not in sections:
                continue
            resid_chunk = {}
            for qr in range(4):
                for half in range(2):
                    cc = qr * 2 + half
                    if XRES_SIMPLE:
                        P.dma("sp", "xres", lambda sp, u=u, qr=qr, half=half: sp.dma_start(
                            out=xres[qr % 2][:, half * 256:(half + 1) * 256],
                            in_=xt[0:128, 0:256]),
                            waits=([xres_free[qr % 2]] if half == 0 else []),
                            tag=f"xres{u}_{qr}_{half}")
                    else:
                        P.dma("act", "xres", lambda a, u=u, qr=qr, half=half, cc=cc: a.dma_start(
                            out=xres[qr % 2][:, half * 256:(half + 1) * 256],
                            in_=xt[cc * 128:(cc + 1) * 128, u * 256:(u + 1) * 256]),
                            waits=([xres_free[qr % 2]] if half == 0 else []),
                            tag=f"xres{u}_{qr}_{half}")
                xres_ld = P.evt("xres")
                for half in range(2):
                    c = qr * 2 + half
                    if RESID_MODE >= 2:
                        P.op("dve", lambda v, u=u, qr=qr, half=half: v.tensor_sub(
                            rs1[:, 0:256], xres[qr % 2][:, half * 256:(half + 1) * 256],
                            mub[u % 2][:, :]),
                            waits=[xres_ld] + ([wo_grp_evt[c]] if wo_grp_evt else []),
                            tag=f"res{u}_{c}")
                    if RESID_MODE >= 3:
                        P.op("dve", lambda v, u=u: v.tensor_mul(
                            rs1[:, 256:512], rs1[:, 0:256], rbf[u % 2][:, :]))
                    if RESID_MODE >= 4:
                        P.op("dve", lambda v, c=c: v.tensor_add(
                            tTb[:, c * 256:(c + 1) * 256], tTb[:, c * 256:(c + 1) * 256],
                            rs1[:, 256:512]))
                        P.op("dve", lambda v, c=c: v.tensor_mul(
                            sq_glob[:, c * 256:(c + 1) * 256],
                            tTb[:, c * 256:(c + 1) * 256],
                            tTb[:, c * 256:(c + 1) * 256]), tag=f"rsq{u}_{c}")
                    if RESID_MODE < 2:
                        P.op("dve", lambda v: v.engine_nop(), waits=[xres_ld])
                    P.extra_inc("dve", "dve")
                    resid_chunk[c] = P.evt("dve")
                xres_free[qr % 2] = P.evt("dve")
            resid_evt = P.evt("dve")

            if 'ln2' not in sections and 'wo' not in sections:
                continue
            ce2 = ([resid_chunk[c] for c in range(8)]
                   if RESID_MODE >= 4 and len(resid_chunk) == 8 else None)
            mu2_evt, bc2_evt = plan_ln_stats(
                tTb, rows, [rows_free], mu2b, r2b, [resid_evt], f"L2_{u}_",
                ([p6_free, av_region_free] if ce2 is not None
                 else [resid_evt, p6_free, av_region_free]),
                chunk_evts=ce2)
            rows_free = bc2_evt
            p6_free = mu2_evt
            for c in range(8):
                hb = (c // 2) % 2
                P.op("dve", lambda v, c=c: v.tensor_sub(
                    rs1[:, 0:256], tTb[:, c * 256:(c + 1) * 256], mu2b[:, :]),
                    waits=([hb_free[hb]] if c % 2 == 0 else []), tag=f"h{u}_{c}")
                P.op("dve", lambda v: v.tensor_mul(rs1[:, 256:512], rs1[:, 0:256], r2b[:, :]))
                P.op("dve", lambda v, c=c, hb=hb: v.tensor_scalar(
                    hbuf[hb][:, (c % 2) * 256:(c % 2) * 256 + 256], rs1[:, 256:512],
                    g1_s[:, c:c + 1], b1_s[:, c:c + 1], ALU.mult, ALU.add))
                if c % 2 == 1:
                    P.extra_inc("dve", "dve")
                    hev = P.evt("dve")
                    P.dma("pool", "spl", lambda gp, u=u, c=c, hb=hb: gp.dma_start(
                        out=hts[u, :, (c - 1) * 256:(c + 1) * 256], in_=hbuf[hb][:, :]),
                        waits=[hev], tag=f"hspill{u}_{c}")
                    hb_free[hb] = P.evt("spl")

    for eng, own in (("pe", "pe"), ("act", "act"), ("dve", "dve"),
                     ("pool", "poolc"), ("sp", "ldx")):
        P.op(eng, lambda e: e.nop(), waits=[P.evt(own)], inc="p1", tag="p1_nop")
    spl_final = P.evt("spl")
    ws_final = P.evt("ws")
    xres_final = P.evt("xres")
    trd_final = P.evt("trd")
    ldx_final = P.evt("ldx")
    cld_final = P.evt("cld")
    ldw_final = P.evt("ldw")

    barrier = [("p1", 5), ws_final, spl_final, xres_final, trd_final,
               ldx_final, cld_final, ldw_final]
    Pf = [PS0[:, 0:256], PS0[:, 512:768], PS1[:, 0:256], PS1[:, 512:768]]
    Pu = [PS2[:, 256:512], PS3[:, 256:512], PS5[:, 256:512]]

    hb_evt2, gelu_evt = {}, {}
    hTb_free = [("pe", 0), ("pe", 0)]
    pu_free = [("act", 0), ("act", 0), ("act", 0)]
    pf_free = [("dve", 0)] * 4
    hq_free = [("dve", 0)] * 4
    thb8_free = ("dve", 0)
    xq_free = [("dve", 0), ("dve", 0)]
    outb_free = [("outs", 0), ("outs", 0)]
    p6b_free = ("dve", 0)
    rows3_free = ("pe", 0)
    yqf_free = [("act", 0), ("dve", 0)]
    rr_evt_d, stats_evt_d, mu_evt_d, tT2_evt_d = {}, {}, {}, {}
    yqf_evt_d, tanh_evt_d = {}, {}
    NT2B = 16

    def load_hb(T):
        P.dma("pool", "ld2g", lambda gp, T=T: gp.dma_start(
            out=hTb2[T % 2][:, :], in_=hts[T]),
            waits=barrier + [hTb_free[T % 2]], tag=f"ldhb{T}")
        hb_evt2[T] = P.evt("ld2g")

    def plan_out_loads(T):
        P.dma("sp", "ld2", lambda sp, T=T: sp.dma_start(out=yqf[:, :], in_=yts[T]),
              waits=barrier + [spl_final]
              + [(s, c + (1 if s in ("dve", "act") else 0)) for (s, c) in yqf_free],
              tag=f"ldyf{T}")
        yqf_evt_d[T] = P.evt("ld2")

    def plan_tanh(T, i):
        w = [yqf_evt_d[T]]
        if i == 0:
            w += [("pe", stats_evt_d[T][1]),
                  (thb8_free[0], thb8_free[1]
                   + (1 if thb8_free[0] == "dve" else 0))]
        P.op("act", lambda a, i=i: a.activation(
            out=thb8[:, i * 256:(i + 1) * 256], in_=yqf[:, i * 256:(i + 1) * 256],
            func=AF.Tanh, scale=0.5), waits=w, inc="act", tag=f"tanh{T}_{i}")
        tanh_evt_d[(T, i)] = P.evt("act")

    def plan_ln_bcast_T(T):
        nonlocal rows3_free, p6b_free
        mu_evt, bc_evt = plan_ln_bcast(rows3, mu3b2[T % 2], r3b2[T % 2],
                                       rr_evt_d[T], [p6b_free], f"L3b{T}_")
        mu_evt_d[T] = mu_evt
        rows3_free = bc_evt
        p6b_free = mu_evt

    def plan_out_chain(T):
        nonlocal thb8_free, yqf_free
        buf, mu, r = tT2b[T % 2], mu3b2[T % 2], r3b2[T % 2]
        mu_evt = mu_evt_d[T]
        gg_evt = ("dve", 0)
        for c in range(8):
            P.dma("sp", "ld2", lambda sp, T=T, c=c: sp.dma_start(
                out=xq[c % 2][:, :], in_=xt[c * 128:(c + 1) * 128, T * 256:(T + 1) * 256]),
                waits=barrier + [(xq_free[c % 2][0], xq_free[c % 2][1]
                                  + (1 if xq_free[c % 2][0] == "dve" else 0))],
                tag=f"ldx2{T}_{c}")
            xq_evt = P.evt("ld2")
            ob = c % 2
            P.op("dve", lambda v, c=c: v.scalar_tensor_tensor(
                rs3[:, 256:512], thb8[:, c * 256:(c + 1) * 256], 1.0,
                yqf[:, c * 256:(c + 1) * 256], ALU.add, ALU.mult),
                waits=[("act", tanh_evt_d[(T, c)][1] + 1)], inc="dve", tag=f"gg{T}_{c}")
            gg_evt = P.evt("dve")
            P.op("dve", lambda v, c=c, buf=buf, mu=mu: v.tensor_sub(
                rs3[:, 0:256], buf[:, c * 256:(c + 1) * 256], mu[:, :]),
                waits=[mu_evt], tag=f"s{T}_{c}")
            P.op("dve", lambda v, r=r, ob=ob: v.tensor_mul(outb[ob][:, :], rs3[:, 0:256], r[:, :]),
                 waits=[outb_free[ob]])
            P.op("dve", lambda v, c=c, ob=ob: v.tensor_scalar(
                rs3[:, 0:256], outb[ob][:, :], g2_s[:, c:c + 1], b2_s[:, c:c + 1],
                ALU.mult, ALU.add))
            P.op("dve", lambda v: v.tensor_mul(rs3[:, 256:512], rs3[:, 0:256], rs3[:, 256:512]))
            P.op("dve", lambda v, c=c, ob=ob: v.scalar_tensor_tensor(
                outb[ob][:, :], rs3[:, 256:512], 0.5, xq[c % 2][:, :], ALU.mult, ALU.add),
                waits=[xq_evt], inc="dve", tag=f"out{T}_{c}")
            out_evt = P.evt("dve")
            xq_free[c % 2] = out_evt
            P.op("dve", lambda v: v.engine_nop(), inc="dve", tag=f"outlag{T}_{c}")
            P.dma("act", "outs", lambda a, T=T, c=c, ob=ob: a.dma_start(
                out=outT[c * 128:(c + 1) * 128, T * 256:(T + 1) * 256], in_=outb[ob][:, :]),
                waits=[("dve", out_evt[1] + 1)], tag=f"st{T}_{c}")
            outb_free[ob] = P.evt("outs")
        thb8_free = gg_evt
        yqf_free = [tanh_evt_d[(T, 7)], gg_evt]

    n_t2 = NT2B if do_pass2 else 0
    w1c_evt, w2c_evt = {}, {}
    if n_t2:
        load_hb(0)
        for ccw in range(8):
            P.dma("sp", "w12", lambda sp, ccw=ccw: sp.dma_start(
                out=w1res[:, ccw * 4096:(ccw + 1) * 4096],
                in_=w1s[ccw * 4:(ccw + 1) * 4].rearrange("a p f -> p a f")),
                waits=barrier + [ws_evt[("w1", ccw * 4 + 3)]], tag=f"ldw1res{ccw}")
            w1c_evt[ccw] = P.evt("w12")
            P.dma("pool", "ld2g", lambda gp, ccw=ccw: gp.dma_start(
                out=w2res[:, ccw * 4096:(ccw + 1) * 4096],
                in_=w2s[ccw * 4:(ccw + 1) * 4].rearrange("a p f -> p a f")),
                waits=barrier + [ws_evt[("w2", ccw * 4 + 3)]], tag=f"ldw2res{ccw}")
            w2c_evt[ccw] = P.evt("ld2g")

    for T in range(n_t2):
        if T + 1 < n_t2:
            load_hb(T + 1)
        if T >= 1:
            plan_out_loads(T - 1)
        for c in range(32):
            pu = Pu[c % 3]
            for d in range(8):
                P.op("pe", lambda pe, c=c, d=d, pu=pu, T=T: pe.matmul(
                    pu, w1res[:, c * 1024 + d * 128: c * 1024 + (d + 1) * 128],
                    hTb2[T % 2][:, d * 256:(d + 1) * 256],
                    start=(d == 0), stop=(d == 7)),
                    waits=(barrier + [w1c_evt[c // 4], hb_evt2[T], pu_free[c % 3]]
                           if d == 0 else []),
                    tag=f"uT{T}_{c}_{d}")
            P.extra_inc("pe", "pe")
            ug_evt = P.evt("pe")
            P.op("act", lambda a, c=c, pu=pu: a.activation(
                out=uTb[:, c * 256:(c + 1) * 256], in_=pu,
                func=AF.Gelu_apprx_tanh, bias=c1_s[:, c:c + 1]),
                waits=[ug_evt], inc="act", tag=f"gelu{T}_{c}")
            gelu_evt[c] = P.evt("act")
            pu_free[c % 3] = ("act", gelu_evt[c][1] + 1)
            if T >= 1 and 12 <= c <= 26 and c % 2 == 0:
                plan_tanh(T - 1, (c - 12) // 2)
        hTb_free[T % 2] = P.evt("pe")
        act_fence(rows3, f"ugelfence{T}")
        if debug:
            P.dma("pool", "outs", lambda gp, T=T: gp.dma_start(
                out=udbg[T], in_=uTb[:, :]), waits=[P.evt("act")], tag=f"udbg{T}")
        if T >= 1:
            plan_ln_bcast_T(T - 1)
            plan_out_chain(T - 1)

        hq_evt = {}
        f_chunk = {}

        def plan_hq_load(dout, T=T):
            P.dma("pool", "ld2g", lambda gp, T=T, dout=dout: gp.dma_start(
                out=hq[dout % 4][:, :], in_=hts[T, :, dout * 256:(dout + 1) * 256]),
                waits=barrier + [spl_final,
                                 (hq_free[dout % 4][0], hq_free[dout % 4][1]
                                  + (1 if hq_free[dout % 4][0] == "dve" else 0))],
                tag=f"ldhq{T}_{dout}")
            hq_evt[dout] = P.evt("ld2g")

        for dout in range(4):
            plan_hq_load(dout)
        for dgrp in range(2):
            for c in range(32):
                for dl in range(4):
                    dout = dgrp * 4 + dl
                    w = []
                    if dl == 0:
                        w = [w2c_evt[c // 4], ("act", gelu_evt[c][1] + 1)]
                        if c == 0:
                            w += [(pf_free[x][0], pf_free[x][1]
                                   + (1 if pf_free[x][0] == "dve" else 0))
                                  for x in range(4)]
                    P.op("pe", lambda pe, c=c, dl=dl, dout=dout: pe.matmul(
                        Pf[dl], w2res[:, c * 1024 + dout * 128: c * 1024 + (dout + 1) * 128],
                        uTb[:, c * 256:(c + 1) * 256],
                        start=(c == 0), stop=(c == 31)),
                        waits=w, tag=f"fT{T}_{c}_{dout}")
                P.extra_inc("pe", "pe")
            P.op("pe", lambda pe: pe.nop(), inc="pe", tag=f"fdrain{T}_{dgrp}")
            fgrp_evt = P.evt("pe")
            for dl in range(4):
                dout = dgrp * 4 + dl
                P.op("dve", lambda v, dl=dl, dout=dout, T=T: v.scalar_tensor_tensor(
                    tT2b[T % 2][:, dout * 256:(dout + 1) * 256], Pf[dl],
                    c2_s[:, dout:dout + 1], hq[dout % 4][:, :],
                    ALU.add, ALU.add),
                    waits=[fgrp_evt, hq_evt[dout], consts_evt], inc="dve",
                    tag=f"fev{T}_{dout}")
                pf_free[dl] = P.evt("dve")
                hq_free[dout % 4] = pf_free[dl]
                if dout + 4 <= 7:
                    plan_hq_load(dout + 4)
                P.op("dve", lambda v, dout=dout, T=T: v.tensor_mul(
                    sqp2[:, dout * 256:(dout + 1) * 256],
                    tT2b[T % 2][:, dout * 256:(dout + 1) * 256],
                    tT2b[T % 2][:, dout * 256:(dout + 1) * 256]),
                    inc="dve", tag=f"fsq{T}_{dout}")
                f_chunk[dout] = P.evt("dve")
            P.op("dve", lambda v: v.engine_nop(), inc="dve", tag=f"fevlag{T}_{dgrp}")
        tT2_evt = P.evt("dve")
        tT2_evt_d[T] = tT2_evt
        if debug:
            P.dma("pool", "outs", lambda gp, T=T: gp.dma_start(
                out=tdbg[T // 2, :, (T % 2) * 2048:(T % 2) * 2048 + 2048],
                in_=tT2b[T % 2][:, :]), waits=[tT2_evt], tag=f"tdbg{T}")

        rr_evt_d[T], stats_evt_d[T] = plan_ln_stats_p1(
            tT2b[T % 2], rows3, [rows3_free], [tT2_evt], f"L3_{T}_",
            [p6b_free], sq_buf=sqp2,
            chunk_evts=[f_chunk[c] for c in range(8)])

    if n_t2:
        Tl = n_t2 - 1
        plan_out_loads(Tl)
        for i in range(8):
            plan_tanh(Tl, i)
        act_fence(rows3, "tailfence")
        plan_ln_bcast_T(Tl)
        plan_out_chain(Tl)
        P.op("dve", lambda v: v.engine_nop(), inc="dve", tag="final_dve_nop")
        out_final = P.evt("outs")
        P.op("pool", lambda gp: gp.engine_nop(), waits=[out_final], tag="final_wait")
    else:
        P.op("pool", lambda gp: gp.engine_nop(), waits=[P.evt("spl")], tag="final_wait")

    nc._dbg_handles = {"uTb": uTb, "tT2": tT2b[0], "tT2b1": tT2b[1], "mu3s": mu3b2[0], "r3s": r3b2[0]}
    nc._dbg_planner = P
    nc._dbg_mem = (SHARED_END, cursor[0], SB_LIMIT)

    engmap = {"pe": "tensor", "act": "scalar", "dve": "vector",
              "pool": "gpsimd", "sp": "sync"}
    with nc.Block() as block:
        def make_runner(engname):
            def run(eng):
                wm = {s: 0 for s in SEMS}
                for fn, waits, incs, tag in P.entries[engname]:
                    emitted_wait = False
                    for (sname, c) in waits:
                        if c > wm[sname]:
                            eng.wait_ge(sems[sname], c)
                            wm[sname] = c
                            emitted_wait = True
                    if engname == "pe" and emitted_wait:
                        eng.nop()
                    inst = fn(eng)
                    for (sname, d) in incs:
                        inst = inst.then_inc(sems[sname], d)
            return run
        block.tensor(make_runner("pe"))
        block.scalar(make_runner("act"))
        block.vector(make_runner("dve"))
        block.gpsimd(make_runner("pool"))
        block.sync(make_runner("sp"))

    return nc


_NC_CACHE = {}


def _host_inputs(inputs):
    f32 = np.float32
    bf = ml_dtypes.bfloat16
    Wq = np.ascontiguousarray(inputs["L0_attn_Wq"], f32)
    Wk = np.ascontiguousarray(inputs["L0_attn_Wk"], f32)
    Wv = np.ascontiguousarray(inputs["L0_attn_Wv"], f32)
    Wo = np.ascontiguousarray(inputs["L0_attn_Wo"], f32)
    W1 = np.ascontiguousarray(inputs["L0_attn_W1"], f32)
    W2 = np.ascontiguousarray(inputs["L0_attn_W2"], f32)
    fwt = np.ascontiguousarray(inputs["L0_ffn_weight"].T, f32)
    disha = np.asarray(inputs["L0_ffn_disha"], f32)

    w1r = np.ascontiguousarray(W1.reshape(8, 128, 32, 128).transpose(2, 0, 1, 3))
    w2r = np.ascontiguousarray(W2.reshape(32, 128, 8, 128).transpose(0, 2, 1, 3))

    dp = np.zeros((128, 1024), f32)
    dp[0:64] = disha

    pp = np.zeros((128, 128), f32)
    for p in range(128):
        pp[p, p % 64] = 1.0
    pp = pp.astype(bf)

    idn = np.eye(128, dtype=f32).astype(bf)

    masks = np.zeros((128, 4 * 256), f32)
    i = np.arange(256)[None, :]
    for slot, kc in enumerate((0, 1, 4, 5)):
        jg = kc * 128 + np.arange(128)[:, None]
        masks[:, slot * 256:(slot + 1) * 256] = (np.abs(i + 256 - jg) <= 256)
    masks = masks.astype(bf)

    def b2d(v, nch):
        return np.ascontiguousarray(np.asarray(v, f32).reshape(nch, 128).T)

    shared = {
        "wq": Wq, "wk": Wk, "wv": Wv, "wo": Wo, "fwt": fwt,
        "w1r": w1r, "w2r": w2r, "dp": dp, "pp": pp, "idn": idn, "masks": masks,
        "bvb_in": np.ascontiguousarray(
            np.broadcast_to(np.asarray(inputs["L0_attn_bv"], f32), (128, 1024))),
        "bq2": b2d(inputs["L0_attn_bq"], 8), "bk2": b2d(inputs["L0_attn_bk"], 8),
        "bo2": b2d(inputs["L0_attn_bo"], 8), "g12": b2d(inputs["L0_attn_g1"], 8),
        "b12": b2d(inputs["L0_attn_b1"], 8), "g22": b2d(inputs["L0_attn_g2"], 8),
        "b22": b2d(inputs["L0_attn_b2"], 8), "c22": b2d(inputs["L0_attn_c2"], 8),
        "c12": b2d(inputs["L0_attn_c1"], 32),
        "ones128": np.ones((128, 1), f32),
        "ones1x128": np.ones((1, 128), f32),
    }
    return shared


def kernel(**inputs):
    x = np.asarray(inputs["L0_attn_x"], np.float32)
    shared = _host_inputs(inputs)
    if "nc" not in _NC_CACHE:
        _NC_CACHE["nc"] = build_program()
    nc = _NC_CACHE["nc"]
    in_maps = []
    for c in range(8):
        m = dict(shared)
        m["xt"] = np.ascontiguousarray(x[c].T)
        in_maps.append(m)
    res = run_bass_kernel_spmd(nc, in_maps, core_ids=list(range(8)))
    out = np.empty((8, 4096, 1024), np.float32)
    for c in range(8):
        out[c] = np.asarray(res.results[c]["outT"]).T
    return out
